# version 23
# baseline (speedup 1.0000x reference)
import numpy as np
from contextlib import ExitStack

import concourse.bass as bass
import concourse.mybir as mybir
from concourse.bass_utils import run_bass_kernel_spmd

F32 = mybir.dt.float32
BF16 = mybir.dt.bfloat16
I32 = mybir.dt.int32
AF = mybir.ActivationFunctionType
ALU = mybir.AluOpType
AXX = mybir.AxisListType.X

L = 2048
C = 2048
H = 5632
D = 96
NST = 16
RK = 6
EPS = 1e-5
NCC = C // 128
NH = H // 128
NTT = L // 128
QT = L // 4
TPQ = 4
NE = 8
TPE = 2
ET = L // NE
HALF = L // 2
TPH = 8
WN = float(H * C)
M2S = [7, 7, 7, 7, 7, 7, 2]
M2MAX = 7


class Sem:
    def __init__(self, es, nc, name):
        self.h = es.enter_context(nc.semaphore(name))
        self.v = 0

    def inc(self, inst, n=1):
        inst.then_inc(self.h, n)
        self.v += n
        return self.v


def build_core_program(stop_after=None):
    nc = bass.Bass("TRN2", target_bir_lowering=False)
    dr = {}

    def din(name, shp, dt=F32, kind="ExternalInput"):
        dr[name] = nc.dram_tensor(name, list(shp), dt, kind=kind).ap()
        return dr[name]

    din("x", (L, C))
    din("w1L", (NH, 128, NCC * 128))
    din("w3L", (NH, 128, NCC * 128))
    din("w2T", (NH, 128, C))
    din("inprojT", (C, D))
    din("outprojT", (D, C))
    din("xprojT", (D, RK + 2 * NST))
    din("dtT", (RK, D))
    din("Alog", (D, NST))
    din("Dp", (D, 1))
    din("convw", (D, 3))
    din("convb", (D, 1))
    din("dtb", (D, 1))
    din("lnw", (128, NCC))
    din("lnb", (128, NCC))
    din("out", (L, C), kind="ExternalOutput")
    din("h_scr", (H, L), kind="Internal")
    din("w2q_scr", (NH, 128, C), BF16, kind="Internal")

    es = ExitStack()
    SBm = lambda n, shp, dt=F32: es.enter_context(nc.sbuf_tensor(n, list(shp), dt))
    MKSEM = lambda n: Sem(es, nc, n)

    ident = SBm("ident", (128, 128))
    ones1_128 = SBm("ones1_128", (1, 128))
    ones1_96 = SBm("ones1_96", (1, 96))
    onescol = SBm("onescol", (128, 1))
    A_sb = SBm("A_sb", (D, NST))
    Dp_sb = SBm("Dp_sb", (D, 1))
    convw_sb = SBm("convw_sb", (D, 3))
    convb_sb = SBm("convb_sb", (D, 1))
    dtb_sb = SBm("dtb_sb", (D, 1))
    lnw_sb = SBm("lnw_sb", (128, NCC))
    lnb_sb = SBm("lnb_sb", (128, NCC))
    outprojT_sb = SBm("outprojT_sb", (D, C))
    xprojT_sb = SBm("xprojT_sb", (D, RK + 2 * NST))
    dtT_sb = SBm("dtT_sb", (RK, D))
    m1c = SBm("m1c", (128, NTT))
    nm1c = SBm("nm1c", (128, NTT))
    r1c = SBm("r1c", (128, NTT))
    stat2 = SBm("stat2", (128, 3 * NTT))
    nm2c = SBm("nm2c", (128, NTT))
    xcol = SBm("xcol", (128, 4 * NTT))
    ssum = SBm("ssum", (128, NTT))
    sq12 = SBm("sq12", (128, 2 * NTT))
    sxinvB = SBm("sxinvB", (128, L))
    y_sb = SBm("y_sb", (D, L))
    mtmp = SBm("mtmp", (128, 4 * NH))
    thrall = SBm("thrall", (128, 9))
    srow = SBm("srow", (1, 16))
    mxc2 = SBm("mxc2", (128, NTT))
    hscol = SBm("hscol", (128, NTT))
    fscr1 = SBm("fscr1", (128, C))
    cns = SBm("cns", (128, 4))
    wstg = [SBm(f"wstg{i}", (128, NCC * 128)) for i in range(2)]
    lnwb = SBm("lnwb", (128, 2 * NCC))

    qint = fscr1[:].bitcast(I32)
    s_ld = MKSEM("s_ld")
    s_x = MKSEM("s_x")
    s_xb = MKSEM("s_xb")
    s_w = [MKSEM("s_w0"), MKSEM("s_w1")]
    s_hd = [MKSEM("s_hd0"), MKSEM("s_hd1")]
    s_wq = MKSEM("s_wq")
    s_w2 = [MKSEM("s_w20"), MKSEM("s_w21")]
    s_wm = [MKSEM("s_wm0"), MKSEM("s_wm1")]
    s_t = MKSEM("s_t")
    s_v = MKSEM("s_v")
    s_a = MKSEM("s_a")
    s_g = MKSEM("s_g")
    s_o = [MKSEM("s_o0"), MKSEM("s_o1")]
    s_fl = MKSEM("s_fl")

    es_qx = ExitStack()
    qx = es_qx.enter_context(nc.sbuf_tensor("qx", [128, NCC * L], BF16,
                                            side="right"))

    es_xin = ExitStack()
    xin = es_xin.enter_context(nc.sbuf_tensor("xin", [D, L], F32))
    wmst = [wstg[0], wstg[1]] + \
        [es_xin.enter_context(nc.sbuf_tensor(f"wmst{i}", [128, NCC * 128],
                                             F32)) for i in range(2)]
    s_wm4 = [MKSEM(f"s_wm4_{i}") for i in range(4)]

    es_ph = ExitStack()
    SBp = lambda n, shp, dt=F32: es_ph.enter_context(nc.sbuf_tensor(n, list(shp), dt))
    iob = SBp("iob", (128, 128))
    inprojT_sb = SBp("inprojT_sb", (128, NCC * D))
    wfold = SBp("wfold", (128, D))
    xh2 = [SBp(f"xh2_{i}", (128, TPE * C)) for i in range(2)]
    pscr = SBp("pscr", (128, C))
    xnTc = SBp("xnTc", (128, 4 * ET))
    rowbuf = SBp("rowbuf", (2 * TPE, 256))
    rows = SBp("rows", (1, 4 * ET))
    sxB = SBp("sxB", (128, ET))
    r2B = SBp("r2B", (D, ET))
    rm2B = SBp("rm2B", (D, ET))
    ipcorr = SBp("ipcorr", (D, ET))
    bvec = SBp("bvec", (D, 2))

    ps_es = ExitStack()
    PS1 = lambda n, shp: ps_es.enter_context(nc.psum_tensor(n, list(shp), F32))
    ps_tr = [PS1(f"ps_tr{i}", (128, ET)) for i in range(4)]
    ps_bc = [PS1(f"ps_bc{i}", (128, ET)) for i in range(2)]
    ps_ip2 = [PS1(f"ps_ip{i}", (D, ET)) for i in range(2)]
    ps_m = ps_bc[0]

    with nc.Block("setup") as blk:
        @blk.sync
        def _(sy):
            for src, dst in (("Alog", A_sb), ("Dp", Dp_sb), ("convw", convw_sb),
                             ("convb", convb_sb), ("dtb", dtb_sb),
                             ("lnw", lnw_sb), ("lnb", lnb_sb),
                             ("outprojT", outprojT_sb), ("xprojT", xprojT_sb),
                             ("dtT", dtT_sb)):
                s_ld.inc(sy.dma_start(out=dst[:], in_=dr[src][:]), 16)
            s_ld.inc(sy.dma_start(
                out=inprojT_sb[:].rearrange("p (a d) -> p a d", a=NCC),
                in_=dr["inprojT"].rearrange("(a p) d -> p a d", p=128)), 16)

        @blk.gpsimd
        def _(g):
            g.iota(iob[:], [[1, 128]], channel_multiplier=-1,
                   allow_small_or_imprecise_dtypes=True)
            g.drain()
            g.tensor_single_scalar(ident[:], iob[:], 0.0, ALU.is_equal)
            g.drain()
            g.memset(ones1_128[:], 1.0)
            g.memset(ones1_96[:], 1.0)
            g.memset(onescol[:], 1.0)
            g.memset(y_sb[:], 0.0)
            g.memset(cns[:, 0:1], EPS)
            g.memset(cns[:, 1:2], 1.0 / C)
            g.memset(cns[:, 2:3], 0.001)
            g.memset(cns[:, 3:4], 1.0)
            g.drain()
            s_g.inc(g.tensor_copy(srow[0:1, 15:16], cns[0:1, 0:1]))
            g.drain()

        @blk.vector
        def _(v):
            v.wait_ge(s_ld.h, s_ld.v)
            v.tensor_copy(lnwb[:].rearrange("p (a two) -> p a two", two=2)[:, :, 0:1],
                          lnb_sb[:].unsqueeze(2))
            v.drain()
            s_v.inc(v.tensor_copy(
                lnwb[:].rearrange("p (a two) -> p a two", two=2)[:, :, 1:2],
                lnw_sb[:].unsqueeze(2)))
            v.drain()

        @blk.scalar
        def _(sc):
            sc.wait_ge(s_ld.h, s_ld.v)
            sc.activation(A_sb[:], A_sb[:], AF.Exp)
            sc.drain()
            s_a.inc(sc.activation(A_sb[:], A_sb[:], AF.Copy, scale=-1.0))
            sc.drain()

        lnwb_ok = s_v.v

        @blk.tensor
        def _(t):
            t.wait_ge(s_ld.h, s_ld.v)
            t.wait_ge(s_v.h, lnwb_ok)
            inst = None
            for cc in range(NCC):
                inst = t.matmul(ps_m[0:D, 0:2],
                                inprojT_sb[:, cc * D:(cc + 1) * D],
                                lnwb[:, 2 * cc:2 * cc + 2],
                                start=(cc == 0), stop=(cc == NCC - 1))
            s_t.inc(inst)
        bvc = s_t.v

        @blk.vector
        def _(v):
            v.wait_ge(s_t.h, bvc)
            v.tensor_copy(bvec[:, 0:2], ps_m[0:D, 0:2])
            v.drain()
            for cc in range(NCC):
                v.tensor_single_scalar(inprojT_sb[:, cc * D:(cc + 1) * D],
                                       inprojT_sb[:, cc * D:(cc + 1) * D],
                                       lnw_sb[:, cc:cc + 1], ALU.mult)
                v.drain()
            s_v.inc(v.tensor_copy(srow[0:1, 14:15], bvec[0:1, 0:1]))
            v.drain()
        fold_ok = s_v.v

    tr_cons = {0: None, 1: None, 2: None, 3: None}

    mean_units = [("w1L", i, i) for i in range(NH)] + \
                 [("w3L", i, NH + i) for i in range(NH)]
    mean_st = {"next": 0, "act": {}, "max": {}}

    def emit_means(blk, count, eng="scalar", defer=False, pre_incs=0):
        j0 = mean_st["next"]
        j1 = min(j0 + count, len(mean_units))
        if j1 == j0:
            return None
        mean_st["next"] = j1
        wld = {}
        sem = s_a if eng == "scalar" else s_v
        for j in range(j0, j1):
            mean_st["act"][j] = (sem, sem.v + pre_incs + (j - j0 + 1))
        mean_st["max"][id(sem)] = (sem, sem.v + pre_incs + (j1 - j0))

        @blk.gpsimd
        def _(g):
            for j in range(j0, j1):
                wname, i, col = mean_units[j]
                par = j % 4
                if j >= 4:
                    psem, pval = mean_st["act"][j - 4]
                    g.wait_ge(psem.h, pval)
                s_wm4[par].inc(g.dma_start(out=wmst[par][:],
                                           in_=dr[wname][i]), 16)
                wld[j] = s_wm4[par].v

        def body_sc(sc):
            for j in range(j0, j1):
                wname, i, col = mean_units[j]
                par = j % 4
                sc.wait_ge(s_wm4[par].h, wld[j])
                s_a.inc(sc.activation(wmst[par][:], wmst[par][:], AF.Abs,
                                      accum_out=mtmp[:, col:col + 1]))
                sc.drain()

        def body_ve(v):
            for j in range(j0, j1):
                wname, i, col = mean_units[j]
                par = j % 4
                v.wait_ge(s_wm4[par].h, wld[j])
                s_v.inc(v.tensor_reduce(mtmp[:, col:col + 1],
                                        wmst[par][:], AXX, ALU.add,
                                        apply_absolute_value=True))
                v.drain()

        body = body_sc if eng == "scalar" else body_ve
        if defer:
            return body
        if eng == "scalar":
            blk.scalar(body_sc)
        else:
            blk.vector(body_ve)
        return None

    def finalize_mean(blk, cols, thr_base, psum_ap, extra_wait=None):
        c0, c1 = cols

        @blk.vector
        def _(v):
            if extra_wait is not None:
                v.wait_ge(s_a.h, extra_wait)
            else:
                for sem, val in mean_st["max"].values():
                    v.wait_ge(sem.h, val)
            s_v.inc(v.tensor_reduce(mtmp[:, c0:c0 + 1], mtmp[:, c0:c1], AXX,
                                    ALU.add))
            v.drain()
        red = s_v.v

        @blk.tensor
        def _(t):
            t.wait_ge(s_v.h, red)
            s_t.inc(t.matmul(psum_ap[0:1, 0:1], mtmp[:, c0:c0 + 1], onescol[:],
                             start=True, stop=True))
        tot = s_t.v

        @blk.vector
        def _(v):
            v.wait_ge(s_t.h, tot)
            v.tensor_single_scalar(srow[0:1, 0:1], psum_ap[0:1, 0:1],
                                   1.0 / WN, ALU.mult)
            v.drain()
            v.tensor_single_scalar(srow[0:1, 1:2], srow[0:1, 0:1], EPS,
                                   ALU.max)
            v.drain()
            v.tensor_single_scalar(srow[0:1, 2:3], srow[0:1, 1:2], 0.5,
                                   ALU.mult)
            v.drain()
            v.tensor_single_scalar(srow[0:1, 3:4], srow[0:1, 1:2], -0.5,
                                   ALU.mult)
            v.drain()
            v.reciprocal(srow[0:1, 4:5], srow[0:1, 1:2])
            v.drain()
            v.reciprocal(srow[0:1, 5:6], srow[0:1, 4:5])
            v.drain()
            s_v.inc(v.tensor_single_scalar(srow[0:1, 5:6], srow[0:1, 5:6],
                                           0.5, ALU.mult))
            v.drain()
        srk = s_v.v

        @blk.tensor
        def _(t):
            t.wait_ge(s_v.h, srk)
            s_t.inc(t.matmul(psum_ap[:, 0:6], ones1_128[:], srow[0:1, 0:6],
                             start=True, stop=True))
        bck = s_t.v

        @blk.vector
        def _(v):
            v.wait_ge(s_t.h, bck)
            v.tensor_copy(thrall[:, thr_base:thr_base + 1], psum_ap[:, 2:3])
            v.drain()
            v.tensor_copy(thrall[:, thr_base + 1:thr_base + 2],
                          psum_ap[:, 3:4])
            v.drain()
            s_v.inc(v.tensor_copy(thrall[:, thr_base + 2:thr_base + 3],
                                  psum_ap[:, 5:6]))
            v.drain()
        return s_v.v

    s_x2 = [s_x, s_xb]
    tp_last = {}
    p7_hist = {}
    evac_hist = {}
    ipd_hist = {}
    trs1_hist = {}
    trs2_hist = {}
    rdma1 = {}
    rdma2 = {}
    bc2c_hist = {}
    x_ok = {}
    tr_cons = {}

    with nc.Block("ph01") as blk:
        def emit_xdma(e):
            @blk.sync
            def _(sy):
                if e >= 2:
                    sy.wait_ge(s_t.h, tp_last[e - 2])
                    sy.wait_ge(s_a.h, p7_hist[e - 2])
                s_x2[e % 2].inc(sy.dma_start(
                    out=xh2[e % 2][:].rearrange("p (a c) -> p a c", a=TPE),
                    in_=dr["x"][e * ET:(e + 1) * ET, :]
                    .rearrange("(a p) c -> p a c", p=128)), 16)
                x_ok[e] = s_x2[e % 2].v

        p3_hist = {}
        p4v_hist = {}
        bc1c_hist = {}
        rb1_hist = {}
        g2s_hist = {}

        def emit_head_a(e):
            t0 = e * TPE
            xb = xh2[e % 2]
            sb1 = 3 * NTT + (e % 2) * 8

            @blk.scalar
            def _(sc):
                sc.wait_ge(s_x2[e % 2].h, x_ok[e])
                for tt in range(TPE):
                    sc.activation(pscr[:], xb[:, tt * C:(tt + 1) * C],
                                  AF.Copy, accum_out=ssum[:, tt:tt + 1])
                    sc.drain()
                sc.activation(m1c[:, t0:t0 + TPE], ssum[:, 0:TPE], AF.Copy,
                              scale=1.0 / C)
                sc.drain()
                sc.activation(nm1c[:, t0:t0 + TPE], ssum[:, 0:TPE],
                              AF.Copy, scale=-1.0 / C)
                sc.drain()
                for tt in range(TPE):
                    sc.activation(pscr[:], xb[:, tt * C:(tt + 1) * C],
                                  AF.Square,
                                  bias=nm1c[:, t0 + tt:t0 + tt + 1],
                                  accum_out=ssum[:, tt:tt + 1])
                    sc.drain()
                s_a.inc(sc.activation(sq12[:, t0:t0 + TPE], ssum[:, 0:TPE],
                                      AF.Sqrt, scale=cns[:, 1:2],
                                      bias=cns[:, 0:1]))
                sc.drain()
            p3_hist[e] = s_a.v

            @blk.vector
            def _(v):
                v.wait_ge(s_a.h, p3_hist[e])
                if e >= 2:
                    v.wait_ge(s_t.h, trs1_hist[e - 2])
                v.reciprocal(r1c[:, t0:t0 + TPE], sq12[:, t0:t0 + TPE])
                v.drain()
                for tt in range(TPE):
                    v.tensor_scalar(xb[:, tt * C:(tt + 1) * C],
                                    xb[:, tt * C:(tt + 1) * C],
                                    m1c[:, t0 + tt:t0 + tt + 1],
                                    r1c[:, t0 + tt:t0 + tt + 1],
                                    ALU.subtract, ALU.mult)
                    v.drain()
                for tt in range(TPE):
                    v.tensor_reduce(
                        stat2[:, 2 * NTT + t0 + tt:2 * NTT + t0 + tt + 1],
                        xb[:, tt * C:(tt + 1) * C], AXX, ALU.max,
                        apply_absolute_value=True)
                    v.drain()
                v.tensor_single_scalar(xcol[:, t0:t0 + TPE],
                                       stat2[:, 2 * NTT + t0:
                                             2 * NTT + t0 + TPE],
                                       EPS, ALU.max)
                v.drain()
                v.reciprocal(xcol[:, t0:t0 + TPE], xcol[:, t0:t0 + TPE])
                v.drain()
                v.tensor_single_scalar(xcol[:, t0:t0 + TPE],
                                       xcol[:, t0:t0 + TPE], 127.0, ALU.mult)
                v.drain()
                v.reciprocal(xcol[:, NTT + t0:NTT + t0 + TPE],
                             xcol[:, t0:t0 + TPE])
                v.drain()
                v.tensor_copy(xcol[:, sb1:sb1 + TPE], xcol[:, t0:t0 + TPE])
                v.drain()
                s_v.inc(v.tensor_copy(xcol[:, sb1 + TPE:sb1 + 2 * TPE],
                                      xcol[:, NTT + t0:NTT + t0 + TPE]))
                v.drain()
            p4v_hist[e] = s_v.v

        def emit_head_b(e):
            sb1 = 3 * NTT + (e % 2) * 8

            @blk.tensor
            def _(t):
                t.wait_ge(s_v.h, p4v_hist[e])
                if e >= 2:
                    t.wait_ge(s_v.h, bc2c_hist[e - 2])
                s_t.inc(t.transpose(ps_bc[0][0:2 * TPE, 0:128],
                                    xcol[:, sb1:sb1 + 2 * TPE], ident[:]))
            trs1_hist[e] = s_t.v

            @blk.vector
            def _(v):
                v.wait_ge(s_t.h, trs1_hist[e])
                if e >= 1:
                    v.wait_ge(s_ld.h, rdma1[e - 1])
                s_v.inc(v.tensor_copy(rowbuf[0:2 * TPE, 0:128],
                                      ps_bc[0][0:2 * TPE, 0:128]))
                v.drain()
            rb1_hist[e] = s_v.v

            @blk.sync
            def _(sy):
                sy.wait_ge(s_v.h, rb1_hist[e])
                for k in range(2):
                    s_ld.inc(sy.dma_start(
                        out=rows[0:1, k * ET:(k + 1) * ET],
                        in_=rowbuf[k * TPE:(k + 1) * TPE, 0:128]), 16)
                rdma1[e] = s_ld.v

            @blk.tensor
            def _(t):
                t.wait_ge(s_ld.h, rdma1[e])
                t.matmul(ps_bc[0][:, 0:ET], ones1_128[:], rows[0:1, 0:ET],
                         start=True, stop=True)
                s_t.inc(t.matmul(ps_bc[1][:, 0:ET], ones1_128[:],
                                 rows[0:1, ET:2 * ET], start=True, stop=True))
            bc1 = s_t.v

            @blk.vector
            def _(v):
                v.wait_ge(s_t.h, bc1)
                v.tensor_copy(sxB[:], ps_bc[0][:, 0:ET])
                v.drain()
                s_v.inc(v.tensor_copy(sxinvB[:, e * ET:(e + 1) * ET],
                                      ps_bc[1][:, 0:ET]))
                v.drain()
            bc1c_hist[e] = s_v.v

        def emit_chunks(e, mid_cb=None):
            t0 = e * TPE
            xb = xh2[e % 2]
            bc1c = bc1c_hist[e]

            @blk.scalar
            def _(sc):
                sc.wait_ge(s_v.h, p4v_hist[e])
                for tt in range(TPE):
                    sc.activation(pscr[:], xb[:, tt * C:(tt + 1) * C],
                                  AF.Copy, accum_out=ssum[:, tt:tt + 1])
                    sc.drain()
                sc.activation(stat2[:, t0:t0 + TPE], ssum[:, 0:TPE], AF.Copy,
                              scale=1.0 / C)
                sc.drain()
                sc.activation(nm2c[:, t0:t0 + TPE], ssum[:, 0:TPE], AF.Copy,
                              scale=-1.0 / C)
                sc.drain()
                for tt in range(TPE):
                    sc.activation(pscr[:], xb[:, tt * C:(tt + 1) * C],
                                  AF.Square,
                                  bias=nm2c[:, t0 + tt:t0 + tt + 1],
                                  accum_out=ssum[:, tt:tt + 1])
                    sc.drain()
                s_a.inc(sc.activation(sq12[:, NTT + t0:NTT + t0 + TPE],
                                      ssum[:, 0:TPE], AF.Sqrt,
                                      scale=cns[:, 1:2], bias=cns[:, 0:1]))
                sc.drain()
            p7_hist[e] = s_a.v

            ipmm = {}
            ps_ip = ps_ip2[e % 2]
            for cc in range(NCC):
                if cc == 10 and mid_cb is not None:
                    mid_cb()
                par = (2, 3, 0, 1)[cc % 4]
                xsl = xnTc[:, cc % 4 * ET:(cc % 4 + 1) * ET]

                @blk.tensor
                def _(t):
                    if tr_cons.get(par) is not None:
                        t.wait_ge(s_v.h, tr_cons[par])
                    if cc == 0:
                        t.wait_ge(s_v.h, p4v_hist[e])
                        if e >= 2:
                            t.wait_ge(s_v.h, evac_hist[e - 2])
                    inst = None
                    for tt in range(TPE):
                        inst = t.transpose(
                            ps_tr[par][:, tt * 128:(tt + 1) * 128],
                            xb[:, tt * C + cc * 128:tt * C + (cc + 1) * 128],
                            ident[:])
                    s_t.inc(inst)
                tp = s_t.v

                @blk.vector
                def _(v):
                    v.wait_ge(s_t.h, tp)
                    if cc >= 4:
                        v.wait_ge(s_t.h, ipmm[cc - 4])
                    s_v.inc(v.tensor_copy(xsl, ps_tr[par][:]))
                    v.drain()
                ev1 = s_v.v
                tr_cons[par] = ev1

                @blk.vector
                def _(v):
                    if cc == 0:
                        v.wait_ge(s_v.h, bc1c)
                    v.tensor_tensor(qint[:, 0:ET], xsl, sxB[:], ALU.mult)
                    v.drain()
                    s_v.inc(v.tensor_copy(
                        qx[:, cc * L + e * ET:cc * L + (e + 1) * ET],
                        qint[:, 0:ET]))
                    v.drain()

                @blk.tensor
                def _(t):
                    t.wait_ge(s_v.h, ev1)
                    inst = t.matmul(ps_ip[:],
                                    inprojT_sb[:, cc * D:(cc + 1) * D],
                                    xsl,
                                    start=(cc == 0), stop=(cc == NCC - 1))
                    s_t.inc(inst)
                ipmm[cc] = s_t.v
            tp_last[e] = ipmm[NCC - 1]
            ipd_hist[e] = ipmm[NCC - 1]

        def emit_tail(e):
            t0 = e * TPE
            sb2 = 3 * NTT + (e % 2) * 8 + 4
            ps_ip = ps_ip2[e % 2]
            psb_free = bc1c_hist[min(e + 1, NE - 1)]

            @blk.vector
            def _(v):
                v.wait_ge(s_a.h, p7_hist[e])
                if e >= 2:
                    v.wait_ge(s_t.h, trs2_hist[e - 2])
                v.reciprocal(stat2[:, NTT + t0:NTT + t0 + TPE],
                             sq12[:, NTT + t0:NTT + t0 + TPE])
                v.drain()
                v.tensor_tensor(xcol[:, 2 * NTT + t0:2 * NTT + t0 + TPE],
                                stat2[:, t0:t0 + TPE],
                                stat2[:, NTT + t0:NTT + t0 + TPE], ALU.mult)
                v.drain()
                v.tensor_copy(xcol[:, sb2:sb2 + TPE],
                              stat2[:, NTT + t0:NTT + t0 + TPE])
                v.drain()
                s_v.inc(v.tensor_copy(xcol[:, sb2 + TPE:sb2 + 2 * TPE],
                                      xcol[:, 2 * NTT + t0:
                                            2 * NTT + t0 + TPE]))
                v.drain()
            g2s_hist[e] = s_v.v

            @blk.tensor
            def _(t):
                t.wait_ge(s_v.h, g2s_hist[e])
                t.wait_ge(s_v.h, psb_free)
                s_t.inc(t.transpose(ps_bc[0][0:2 * TPE, 0:128],
                                    xcol[:, sb2:sb2 + 2 * TPE], ident[:]))
            trs2_hist[e] = s_t.v

            @blk.vector
            def _(v):
                v.wait_ge(s_t.h, trs2_hist[e])
                if e >= 1:
                    v.wait_ge(s_ld.h, rdma2[e - 1])
                s_v.inc(v.tensor_copy(rowbuf[0:2 * TPE, 128:256],
                                      ps_bc[0][0:2 * TPE, 0:128]))
                v.drain()
            rb2 = s_v.v

            @blk.sync
            def _(sy):
                sy.wait_ge(s_v.h, rb2)
                for k in range(2):
                    s_ld.inc(sy.dma_start(
                        out=rows[0:1, (2 + k) * ET:(3 + k) * ET],
                        in_=rowbuf[k * TPE:(k + 1) * TPE, 128:256]), 16)
                rdma2[e] = s_ld.v

            @blk.tensor
            def _(t):
                t.wait_ge(s_ld.h, rdma2[e])
                t.matmul(ps_bc[0][0:D, 0:ET], ones1_96[:],
                         rows[0:1, 2 * ET:3 * ET], start=True, stop=True)
                s_t.inc(t.matmul(ps_bc[1][0:D, 0:ET], ones1_96[:],
                                 rows[0:1, 3 * ET:4 * ET], start=True,
                                 stop=True))
            bc2 = s_t.v

            @blk.vector
            def _(v):
                v.wait_ge(s_t.h, bc2)
                v.tensor_copy(r2B[:], ps_bc[0][0:D, 0:ET])
                v.drain()
                s_v.inc(v.tensor_copy(rm2B[:], ps_bc[1][0:D, 0:ET]))
                v.drain()
            bc2c_hist[e] = s_v.v

            @blk.vector
            def _(v):
                v.wait_ge(s_t.h, ipd_hist[e])
                v.tensor_tensor(fscr1[0:D, 0:ET], ps_ip[:], r2B[:], ALU.mult)
                v.drain()
                v.tensor_scalar(ipcorr[:], rm2B[:], bvec[:, 1:2],
                                bvec[:, 0:1], ALU.mult, ALU.subtract)
                v.drain()
                s_v.inc(v.tensor_tensor(xin[:, e * ET:(e + 1) * ET],
                                        fscr1[0:D, 0:ET], ipcorr[:],
                                        ALU.subtract))
                v.drain()
            evac_hist[e] = s_v.v

        emit_xdma(0)
        emit_xdma(1)
        emit_head_a(0)
        emit_head_b(0)
        for e in range(NE):
            if e >= 1:
                emit_tail(e - 1)
            cb = (lambda e=e: emit_head_a(e + 1)) if e + 1 < NE else None
            emit_chunks(e, cb)
            if e + 2 < NE:
                emit_xdma(e + 2)
            if e + 1 < NE:
                emit_head_b(e + 1)
            if e >= 2:
                emit_means(blk, 2, "scalar")
        emit_tail(NE - 1)

    ps_es.close()
    es_ph.close()
    xin_ok = evac_hist[NE - 1]
    if stop_after == "ph1":
        es_xin.close(); es_qx.close(); es.close()
        return nc

    es_sc = ExitStack()
    SBs = lambda n, shp, dt=F32: es_sc.enter_context(nc.sbuf_tensor(n, list(shp), dt))
    sel = SBs("sel", (NST, NST * D))
    ucv = SBs("ucv", (D, L))
    bp_sb = SBs("bp_sb", (NST, L))
    cp_sb = SBs("cp_sb", (NST, L))
    delta = SBs("delta", (D, L))
    du = SBs("du", (D, L))
    dAb = [SBs(f"dAb{i}", (D, L)) for i in range(2)] + [fscr1]
    dbc_dr = delta
    dA_n = dAb[0]
    dBu_n = ucv
    st_n = dBu_n
    yprod = xin

    ps_es2 = ExitStack()
    PS2 = lambda n, shp: ps_es2.enter_context(nc.psum_tensor(n, list(shp), F32))
    ps_sc = PS2("ps_sc", (D, L))
    ps_cc = PS2("ps_cc", (D, L))

    with nc.Block("attn") as blk:
        @blk.gpsimd
        def _(g):
            g.iota(sel[:], [[1, NST], [0, D]], channel_multiplier=-1,
                   allow_small_or_imprecise_dtypes=True)
            g.drain()
            s_g.inc(g.tensor_single_scalar(sel[:], sel[:], 0.0, ALU.is_equal))
            g.drain()
        sel_ok = s_g.v

        @blk.vector
        def _(v):
            v.wait_ge(s_v.h, xin_ok)
            s_v.inc(v.tensor_scalar(ucv[:], xin[:], convw_sb[:, 1:2],
                                    convb_sb[:], ALU.mult, ALU.add))
            v.drain()
        c1d = s_v.v

        @blk.vector
        def _(v):
            v.wait_ge(s_v.h, c1d)
            v.scalar_tensor_tensor(ucv[:, 1:L], xin[:, 0:L - 1],
                                   convw_sb[:, 0:1], ucv[:, 1:L],
                                   ALU.mult, ALU.add)
            v.drain()
            s_v.inc(v.scalar_tensor_tensor(ucv[:, 0:L - 1], xin[:, 1:L],
                                           convw_sb[:, 2:3], ucv[:, 0:L - 1],
                                           ALU.mult, ALU.add))
            v.drain()
        cvd = s_v.v

        @blk.scalar
        def _(sc):
            sc.wait_ge(s_v.h, cvd)
            s_a.inc(sc.activation(ucv[:], ucv[:], AF.Silu))
            sc.drain()
        ud = s_a.v
        emit_means(blk, 4, "scalar")

        xp_ev = {}
        for si, (c0, w, dst) in enumerate(((0, RK, dbc_dr),
                                           (RK, NST, bp_sb),
                                           (RK + NST, NST, cp_sb))):
            @blk.tensor
            def _(t):
                t.wait_ge(s_a.h, ud)
                if si > 0:
                    t.wait_ge(s_v.h, xp_ev[si - 1])
                inst = None
                for j in range(4):
                    inst = t.matmul(ps_sc[0:w, j * 512:(j + 1) * 512],
                                    xprojT_sb[:, c0:c0 + w],
                                    ucv[:, j * 512:(j + 1) * 512],
                                    start=True, stop=True)
                s_t.inc(inst)
            mmd = s_t.v

            @blk.vector
            def _(v):
                v.wait_ge(s_t.h, mmd)
                s_v.inc(v.tensor_copy(dst[0:w, :], ps_sc[0:w, :]))
                v.drain()
            xp_ev[si] = s_v.v

        @blk.tensor
        def _(t):
            t.wait_ge(s_v.h, xp_ev[2])
            inst = None
            for j in range(4):
                inst = t.matmul(ps_sc[0:D, j * 512:(j + 1) * 512], dtT_sb[:],
                                dbc_dr[0:RK, j * 512:(j + 1) * 512],
                                start=True, stop=True)
            s_t.inc(inst)
        dtd = s_t.v

        sps = dAb[1]

        @blk.vector
        def _(v):
            v.wait_ge(s_t.h, dtd)
            s_v.inc(v.tensor_single_scalar(sps[:], ps_sc[0:D, :], dtb_sb[:],
                                           ALU.add))
            v.drain()
        zd = s_v.v

        @blk.scalar
        def _(sc):
            sc.wait_ge(s_v.h, zd)
            sc.activation(yprod[:], sps[:], AF.Relu)
            sc.drain()
            sc.activation(dA_n[:], sps[:], AF.Abs)
            sc.drain()
            sc.activation(dA_n[:], dA_n[:], AF.Exp, scale=-1.0)
            sc.drain()
            s_a.inc(sc.activation(dA_n[:], dA_n[:], AF.Ln, bias=cns[0:D, 3:4]))
            sc.drain()
        spd = s_a.v

        @blk.vector
        def _(v):
            v.wait_ge(s_a.h, spd)
            s_v.inc(v.tensor_tensor(sps[:], yprod[:], dA_n[:], ALU.add))
            v.drain()
        sp2d = s_v.v

        @blk.scalar
        def _(sc):
            sc.wait_ge(s_v.h, sp2d)
            s_a.inc(sc.activation(delta[:], sps[:], AF.Copy,
                                  scale=cns[0:D, 2:3]))
            sc.drain()
        dltd = s_a.v

        @blk.vector
        def _(v):
            v.wait_ge(s_a.h, dltd)
            v.tensor_tensor(du[:], delta[:], ucv[:], ALU.mult)
            v.drain()
            s_v.inc(v.scalar_tensor_tensor(y_sb[:], ucv[:], Dp_sb[:], y_sb[:],
                                           ALU.mult, ALU.add))
            v.drain()
        dud = s_v.v

        dAd = {}

        def emit_exp(sc, n):
            s_a.inc(sc.activation(dAb[n % 3][0:D, :], delta[:], AF.Exp,
                                  scale=A_sb[:, n:n + 1]))
            sc.drain()
            dAd[n] = s_a.v

        @blk.scalar
        def _(sc):
            for n in range(3):
                emit_exp(sc, n)
        emit_means(blk, 8, "scalar")

        prev_dBu = None
        prev_yadd = None
        yadds = {}
        for n in range(NST):
            @blk.tensor
            def _(t):
                if n == 0:
                    t.wait_ge(s_v.h, dud)
                    t.wait_ge(s_g.h, sel_ok)
                else:
                    t.wait_ge(s_v.h, prev_dBu)
                inst = None
                for j in range(4):
                    inst = t.matmul(ps_sc[0:D, j * 512:(j + 1) * 512],
                                    sel[:, n * D:(n + 1) * D],
                                    bp_sb[:, j * 512:(j + 1) * 512],
                                    start=True, stop=True)
                s_t.inc(inst)
            bBd = s_t.v

            @blk.vector
            def _(v):
                v.wait_ge(s_t.h, bBd)
                s_v.inc(v.tensor_tensor(dBu_n[:], du[:], ps_sc[0:D, :],
                                        ALU.mult))
                v.drain()
            prev_dBu = s_v.v

            @blk.tensor
            def _(t):
                if prev_yadd is not None:
                    t.wait_ge(s_v.h, prev_yadd)
                inst = None
                for j in range(4):
                    inst = t.matmul(ps_cc[0:D, j * 512:(j + 1) * 512],
                                    sel[:, n * D:(n + 1) * D],
                                    cp_sb[:, j * 512:(j + 1) * 512],
                                    start=True, stop=True)
                s_t.inc(inst)
            bCd = s_t.v

            @blk.vector
            def _(v):
                v.wait_ge(s_a.h, dAd[n])
                v.tensor_tensor_scan(st_n[:], dAb[n % 3][0:D, :], dBu_n[:],
                                     0.0, ALU.mult, ALU.add)
                v.drain()
                v.wait_ge(s_t.h, bCd)
                v.tensor_tensor(yprod[:], st_n[:], ps_cc[0:D, :], ALU.mult)
                v.drain()
                s_v.inc(v.tensor_tensor(y_sb[:], y_sb[:], yprod[:], ALU.add))
                v.drain()
            prev_yadd = s_v.v
            yadds[n] = s_v.v

            mbody = emit_means(blk, 4, "scalar", defer=True,
                               pre_incs=(1 if n + 3 < NST else 0))
            if n + 3 < NST or mbody is not None:
                @blk.scalar
                def _(sc):
                    sc.wait_ge(s_v.h, yadds[n])
                    if n + 3 < NST:
                        emit_exp(sc, n + 3)
                    if mbody is not None:
                        mbody(sc)

        emit_means(blk, len(mean_units) - mean_st["next"])

    ps_es2.close()
    es_sc.close()
    es_xin.close()
    if stop_after == "scan":
        es_qx.close(); es.close()
        return nc

    es_mm = ExitStack()
    SBmm = lambda n, shp, dt=F32: es_mm.enter_context(nc.sbuf_tensor(n, list(shp), dt))
    fscr2 = SBmm("fscr2", (128, C))
    hbuf = [SBmm(f"hbuf{i}", (128, HALF)) for i in range(2)]
    wstg2 = [SBmm(f"wstg2{i}", (128, NCC * 128)) for i in range(2)]
    shB = SBmm("shB", (128, L))

    if stop_after == "means":
        es_mm.close(); es_qx.close(); es.close()
        return nc

    es_m13 = ExitStack()
    w1q = [es_m13.enter_context(nc.sbuf_tensor(f"w1q{i}", [128, NCC * 128], BF16))
           for i in range(2)]
    w2m = [es_m13.enter_context(nc.sbuf_tensor(f"w2m{i}", [128, HALF], F32))
           for i in range(2)]
    gscr = es_m13.enter_context(nc.sbuf_tensor("gscr", [128, NCC * 128], BF16))
    gscr2 = es_m13.enter_context(nc.sbuf_tensor("gscr2", [128, NCC * 128], BF16))
    gscr3 = es_m13.enter_context(nc.sbuf_tensor("gscr3", [128, NCC * 128], BF16))
    gscr4 = es_m13.enter_context(nc.sbuf_tensor("gscr4", [128, NCC * 128], BF16))
    w3q = [es_m13.enter_context(nc.sbuf_tensor(f"w3q{i}", [128, NCC * 128], BF16))
           for i in range(2)]
    hmax = es_m13.enter_context(nc.sbuf_tensor("hmax", [128, L], F32))

    ps_es3 = ExitStack()
    PS3 = lambda n, shp: ps_es3.enter_context(nc.psum_tensor(n, list(shp), F32))
    ps_a = [PS3(f"ps_a{i}", (128, HALF)) for i in range(2)]
    ps_b = [PS3(f"ps_b{i}", (128, HALF)) for i in range(2)]

    with nc.Block("m13") as blk:
        @blk.gpsimd
        def _(g):
            s_g.inc(g.memset(hmax[:], 0.0))
            g.drain()
        hmz = s_g.v

        thr13 = finalize_mean(blk, (0, NH), 0, ps_a[0])
        thr13 = finalize_mean(blk, (NH, 2 * NH), 3, ps_a[0])

        quant_done = {}
        quant_vec = {}
        evac_done = {}
        hbuf_free = {}
        mm_done_per_ht = {}
        w2m_act = {}
        w2m_sems = {}
        w3add_hist = {}
        w1add_hist = {}

        def emit_quant(ht):
            par = ht % 2

            @blk.sync
            def _(sy):
                if ht >= 2:
                    sy.wait_ge(s_v.h, quant_done[ht - 2])
                s_w[par].inc(sy.dma_start(out=wstg[par][:],
                                          in_=dr["w1L"][ht]), 16)
                s_w2[par].inc(sy.dma_start(out=wstg2[par][:],
                                           in_=dr["w3L"][ht]), 16)
            w1ld = s_w[par].v
            w3ld = s_w2[par].v

            @blk.scalar
            def _(sc):
                sc.wait_ge(s_w[par].h, w1ld)
                if ht < 2:
                    sc.wait_ge(s_v.h, thr13)
                if ht >= 1:
                    sc.wait_ge(s_v.h, w1add_hist[ht - 1])
                sc.activation(gscr[:], wstg[par][:], AF.Sign,
                              bias=thrall[:, 1:2])
                sc.drain()
                s_a.inc(sc.activation(gscr2[:], wstg[par][:], AF.Sign,
                                      bias=thrall[:, 0:1]))
                sc.drain()
                if ht >= 1:
                    sc.wait_ge(s_v.h, w3add_hist[ht - 1])
                sc.wait_ge(s_w2[par].h, w3ld)
                sc.activation(gscr3[:], wstg2[par][:], AF.Sign,
                              bias=thrall[:, 4:5])
                sc.drain()
                s_a.inc(sc.activation(gscr4[:], wstg2[par][:], AF.Sign,
                                      bias=thrall[:, 3:4]))
                sc.drain()
            w3sg = s_a.v
            w1sg = w3sg - 1

            @blk.vector
            def _(v):
                v.wait_ge(s_a.h, w1sg)
                if ht >= 2:
                    v.wait_ge(s_t.h, mm_done_per_ht[ht - 2])
                s_v.inc(v.tensor_tensor(w1q[par][:], gscr[:], gscr2[:],
                                        ALU.add))
                v.drain()
            w1add = s_v.v
            w1add_hist[ht] = s_v.v

            @blk.vector
            def _(v):
                v.wait_ge(s_a.h, w3sg)
                s_v.inc(v.tensor_tensor(w3q[par][:], gscr3[:], gscr4[:],
                                        ALU.add))
                v.drain()
            quant_done[ht] = s_v.v
            quant_vec[ht] = s_v.v
            w3add_hist[ht] = s_v.v

        def emit_units(ht):
            par = ht % 2
            for th in range(2):
                u = 2 * ht + th
                pu = u % 2

                @blk.tensor
                def _(t):
                    t.wait_ge(s_v.h, quant_done[ht])
                    if u >= 2:
                        t.wait_ge(s_v.h, evac_done[u - 2])
                    for cc in range(NCC):
                        for j in range(2):
                            t.matmul(ps_a[pu][:, j * 512:(j + 1) * 512],
                                     w1q[par][:, cc * 128:(cc + 1) * 128],
                                     qx[:, cc * L + th * HALF + j * 512:
                                        cc * L + th * HALF + (j + 1) * 512],
                                     start=(cc == 0), stop=(cc == NCC - 1))
                    inst = None
                    for cc in range(NCC):
                        for j in range(2):
                            inst = t.matmul(
                                ps_b[pu][:, j * 512:(j + 1) * 512],
                                w3q[par][:, cc * 128:(cc + 1) * 128],
                                qx[:, cc * L + th * HALF + j * 512:
                                   cc * L + th * HALF + (j + 1) * 512],
                                start=(cc == 0), stop=(cc == NCC - 1))
                    s_t.inc(inst)
                mmd = s_t.v
                mm_done_per_ht[ht] = mmd

                @blk.vector
                def _(v):
                    v.wait_ge(s_t.h, mmd)
                    if pu in hbuf_free:
                        v.wait_ge(s_hd[pu].h, hbuf_free[pu][0])
                        v.wait_ge(s_v.h, hbuf_free[pu][1])
                    s_v.inc(v.tensor_tensor(
                        fscr1[:, 0:HALF], ps_a[pu][:],
                        sxinvB[:, th * HALF:(th + 1) * HALF], ALU.mult))
                    v.drain()
                ta_ok = s_v.v

                @blk.scalar
                def _(sc):
                    sc.wait_ge(s_v.h, ta_ok)
                    s_a.inc(sc.activation(fscr2[:, 0:HALF], fscr1[:, 0:HALF],
                                          AF.Silu, scale=thrall[:, 2:3]))
                    sc.drain()
                sld = s_a.v

                @blk.vector
                def _(v):
                    v.wait_ge(s_a.h, sld)
                    v.tensor_tensor(fscr1[:, 0:HALF], ps_b[pu][:],
                                    sxinvB[:, th * HALF:(th + 1) * HALF],
                                    ALU.mult)
                    v.drain()
                    s_v.inc(v.scalar_tensor_tensor(
                        hbuf[pu][:], fscr1[:, 0:HALF], thrall[:, 5:6],
                        fscr2[:, 0:HALF], ALU.mult, ALU.mult))
                    v.drain()
                evac_done[u] = s_v.v
                h_ok = s_v.v

                @blk.gpsimd
                def _(g):
                    g.wait_ge(s_v.h, h_ok)
                    s_hd[pu].inc(g.dma_start(
                        out=dr["h_scr"][ht * 128:(ht + 1) * 128,
                                        th * HALF:(th + 1) * HALF],
                        in_=hbuf[pu][:]), 16)

                @blk.vector
                def _(v):
                    if u == 0:
                        v.wait_ge(s_g.h, hmz)
                    v.scalar_tensor_tensor(fscr1[:, 0:HALF], hbuf[pu][:],
                                           -1.0, hbuf[pu][:],
                                           ALU.mult, ALU.max)
                    v.drain()
                    s_v.inc(v.tensor_tensor(
                        hmax[:, th * HALF:(th + 1) * HALF],
                        hmax[:, th * HALF:(th + 1) * HALF],
                        fscr1[:, 0:HALF], ALU.max))
                    v.drain()
                hbuf_free[pu] = (s_hd[pu].v, s_v.v)

        def emit_w2mean(ht):
            for hf in range(2):
                j = 2 * ht + hf
                wpar = j % 2

                @blk.sync
                def _(sy):
                    if j >= 2:
                        sy.wait_ge(s_a.h, w2m_act[j - 2])
                    s_wm[wpar].inc(sy.dma_start(
                        out=w2m[wpar][:],
                        in_=dr["w2T"][ht][:, hf * HALF:(hf + 1) * HALF]), 16)
                w2m_sems[j] = s_wm[wpar].v

                @blk.scalar
                def _(sc):
                    sc.wait_ge(s_wm[wpar].h, w2m_sems[j])
                    s_a.inc(sc.activation(
                        w2m[wpar][:], w2m[wpar][:], AF.Abs,
                        accum_out=mtmp[:, (2 + hf) * NH + ht:
                                       (2 + hf) * NH + ht + 1]))
                    sc.drain()
                w2m_act[j] = s_a.v

        for ht in range(NH + 1):
            if ht < NH:
                emit_quant(ht)
            if ht >= 1:
                emit_units(ht - 1)
            if ht < NH:
                emit_w2mean(ht)

    hstore_total = [s_hd[0].v, s_hd[1].v]
    es_qx.close()
    if stop_after == "m13":
        ps_es3.close(); es_m13.close(); es_mm.close(); es.close()
        return nc

    ps_es3.close()

    es_hf = ExitStack()
    rowb2 = es_hf.enter_context(nc.sbuf_tensor("rowb2", [NTT, 128], F32))
    rows2 = es_hf.enter_context(nc.sbuf_tensor("rows2", [1, 2 * L], F32))
    ps_es4 = ExitStack()
    PS4 = lambda n, shp: ps_es4.enter_context(nc.psum_tensor(n, list(shp), F32))
    ps_op = PS4("ps_op", (128, C))
    ps_f = [PS4(f"ps_f{i}", (128, HALF)) for i in range(2)]

    with nc.Block("hfin") as blk:
        thrw2 = finalize_mean(blk, (2 * NH, 4 * NH), 6, ps_op,
                              extra_wait=w2m_act[2 * NH - 1])

        @blk.tensor
        def _(t):
            t.wait_ge(s_v.h, thrw2)
            inst = None
            for j in range(NTT):
                inst = t.transpose(ps_op[:, j * 128:(j + 1) * 128],
                                   hmax[:, j * 128:(j + 1) * 128], ident[:])
            s_t.inc(inst)
        hts = s_t.v

        @blk.vector
        def _(v):
            v.wait_ge(s_t.h, hts)
            for j in range(NTT):
                v.tensor_reduce(mxc2[:, j:j + 1],
                                ps_op[:, j * 128:(j + 1) * 128], AXX, ALU.max)
                v.drain()
            v.tensor_single_scalar(hscol[:], mxc2[:], EPS, ALU.max)
            v.drain()
            v.reciprocal(hscol[:], hscol[:])
            v.drain()
            v.tensor_single_scalar(hscol[:], hscol[:], 127.0, ALU.mult)
            v.drain()
            v.reciprocal(hscol[:], hscol[:])
            v.drain()
            s_v.inc(v.tensor_single_scalar(hscol[:], hscol[:],
                                           thrall[:, 8:9], ALU.mult))
            v.drain()
        hcol_ok = s_v.v

        @blk.tensor
        def _(t):
            t.wait_ge(s_v.h, hcol_ok)
            s_t.inc(t.transpose(ps_op[0:NTT, 0:128], mxc2[:], ident[:]))
        ht2 = s_t.v

        @blk.vector
        def _(v):
            v.wait_ge(s_t.h, ht2)
            s_v.inc(v.tensor_copy(rowb2[:], ps_op[0:NTT, 0:128]))
            v.drain()
        rb2 = s_v.v

        @blk.gpsimd
        def _(g):
            g.wait_ge(s_v.h, rb2)
            s_fl.inc(g.dma_start(out=rows2[0:1, 0:L], in_=rowb2[:]), 16)
        r2ok = s_fl.v

        @blk.vector
        def _(v):
            v.wait_ge(s_fl.h, r2ok)
            v.tensor_single_scalar(rows2[0:1, L:2 * L], rows2[0:1, 0:L], EPS,
                                   ALU.max)
            v.drain()
            v.reciprocal(rows2[0:1, L:2 * L], rows2[0:1, L:2 * L])
            v.drain()
            s_v.inc(v.tensor_single_scalar(rows2[0:1, L:2 * L],
                                           rows2[0:1, L:2 * L], 127.0,
                                           ALU.mult))
            v.drain()
        shrow_ok = s_v.v

        @blk.tensor
        def _(t):
            t.wait_ge(s_v.h, shrow_ok)
            inst = None
            for j in range(4):
                inst = t.matmul(ps_op[:, j * 512:(j + 1) * 512], ones1_128[:],
                                rows2[0:1, L + j * 512:L + (j + 1) * 512],
                                start=True, stop=True)
            s_t.inc(inst)
        shb = s_t.v

        @blk.vector
        def _(v):
            v.wait_ge(s_t.h, shb)
            s_v.inc(v.tensor_copy(shB[:], ps_op[:]))
            v.drain()
    shB_ok = s_v.v
    es_hf.close()
    es_m13.close()

    M2SB = 4
    NSB2 = 11
    es_m2 = ExitStack()
    f_half = es_m2.enter_context(nc.sbuf_tensor("f_half", [128, TPH * C], F32))
    w2s = [es_m2.enter_context(nc.sbuf_tensor(f"w2s{i}", [128, M2SB * C], BF16))
           for i in range(2)]
    qhs = [es_m2.enter_context(nc.sbuf_tensor(f"qhs{i}", [128, M2SB * HALF],
                                              BF16)) for i in range(2)]
    w4stg = [wstg[0], wstg[1], sxinvB]
    s_wx = [MKSEM("s_wx0")]
    s_w4s = [s_w[0], s_w[1], s_wx[0]]

    xre2 = [wstg2[0], wstg[0]]
    gbuf2 = [wstg2[1], wstg[1]]

    wq_sems = {}
    hq_last = {0: None, 1: None}
    prev_store = [0, 0]
    for half in range(2):
        hq_sems = {}
        hm1_sems = {}
        with nc.Block(f"m2_{half}") as blk:
            mm_last = {}
            slab_ok = {}
            prev_sb_cons = {}

            prep_late = {}

            def emit_prep_early(sbi):
                sp = sbi % 2
                sbn = min(M2SB, NH - sbi * M2SB)
                base = sbi * M2SB
                sgb = lambda ht: (fscr2[:].bitcast(BF16) if ht % 2 == 0
                                  else wstg2[0][:].bitcast(BF16))
                slab_v = 0
                slab_w = [0, 0]

                nsg = sbn if half == 0 else 0
                a0 = s_a.v
                v0h = s_v.v
                for k in range(sbn):
                    if k < 2:
                        hq_sems[base + k] = a0 + k + 1
                    else:
                        hq_sems[base + k] = a0 + 2 + nsg + (k - 1)
                    hm1_sems[base + k] = v0h + k + 1
                if half == 0:
                    for k in range(min(2, sbn)):
                        wq_sems[base + k] = v0h + sbn + k + 1

                hlds = {}

                @blk.gpsimd
                def _(g):
                    for k in range(sbn):
                        ht = base + k
                        par = ht % 2
                        if ht >= 2:
                            g.wait_ge(s_v.h, hm1_sems[ht - 2])
                        elif half == 0:
                            g.wait_ge(s_hd[par].h, hstore_total[par])
                        else:
                            g.wait_ge(s_a.h, hq_last[par])
                        s_hd[par].inc(g.dma_start(
                            out=hbuf[par][:],
                            in_=dr["h_scr"][ht * 128:(ht + 1) * 128,
                                            half * HALF:(half + 1) * HALF]),
                            16)
                        hlds[k] = s_hd[par].v

                wlds = {}
                if half == 0:
                    @blk.sync
                    def _(sy):
                        for k in range(sbn):
                            ht = base + k
                            p4 = ht % 3
                            if ht >= 3:
                                sy.wait_ge(s_v.h, wq_sems[ht - 3])
                            s_w4s[p4].inc(sy.dma_start(
                                out=w4stg[p4][:], in_=dr["w2T"][ht]), 16)
                            wlds[k] = s_w4s[p4].v
                else:
                    @blk.sync
                    def _(sy):
                        for k in range(sbn):
                            ht = base + k
                            if sbi == 0 and k == 0:
                                sy.wait_ge(s_wq.h, s_wq.v)
                            if k == 0 and sbi >= 2:
                                sy.wait_ge(s_t.h, mm_last[sbi - 2])
                            s_w[sp].inc(sy.dma_start(
                                out=w2s[sp][:, k * C:(k + 1) * C],
                                in_=dr["w2q_scr"][ht]), 16)
                    slab_w[sp] = s_w[sp].v
                    prep_late[sbi] = None

                hm1s = {}

                @blk.vector
                def _(v):
                    for k in range(sbn):
                        ht = base + k
                        par = ht % 2
                        qv = (qint[:, 0:HALF] if ht % 2 == 0
                              else qint[:, HALF:2 * HALF])
                        v.wait_ge(s_hd[par].h, hlds[k])
                        if ht >= 2:
                            v.wait_ge(s_a.h, hq_sems[ht - 2])
                        if half == 0 and ht == 0:
                            v.wait_ge(s_v.h, shB_ok)
                        got = s_v.inc(v.tensor_tensor(
                            qv, hbuf[par][:],
                            shB[:, half * HALF:(half + 1) * HALF],
                            ALU.mult))
                        v.drain()
                        assert got == hm1_sems[ht]
                        hm1s[k] = got

                def emit_copies(kr):
                    @blk.scalar
                    def _(sc):
                        for k in kr:
                            ht = base + k
                            qv = (qint[:, 0:HALF] if ht % 2 == 0
                                  else qint[:, HALF:2 * HALF])
                            sc.wait_ge(s_v.h, hm1s[k])
                            if k == 0 and sbi >= 2:
                                sc.wait_ge(s_t.h, mm_last[sbi - 2])
                            got = s_a.inc(sc.activation(
                                qhs[sp][:, k * HALF:(k + 1) * HALF], qv,
                                AF.Copy))
                            sc.drain()
                            assert got == hq_sems[ht]

                emit_copies(range(0, min(2, sbn)))

                if half == 0:
                    sgs = {}

                    @blk.scalar
                    def _(sc):
                        for k in range(sbn):
                            ht = base + k
                            p4 = ht % 3
                            sc.wait_ge(s_w4s[p4].h, wlds[k])
                            if ht >= 2 and (ht - 2) in wq_sems:
                                sc.wait_ge(s_v.h, wq_sems[ht - 2])
                            sc.activation(sgb(ht)[:, 0:C], w4stg[p4][:],
                                          AF.Sign, bias=thrall[:, 7:8])
                            sc.drain()
                            got = s_a.inc(sc.activation(
                                sgb(ht)[:, C:2 * C], w4stg[p4][:], AF.Sign,
                                bias=thrall[:, 6:7]))
                            sc.drain()
                            sgs[k] = got

                if sbn > 2:
                    emit_copies(range(2, sbn))
                slab_a = hq_sems[base + sbn - 1]

                if half == 0:
                    for k in range(min(2, sbn)):
                        assert wq_sems[base + k] == s_v.v + k + 1

                    def emit_adds(kr):
                        @blk.vector
                        def _(v):
                            for k in kr:
                                ht = base + k
                                v.wait_ge(s_a.h, sgs[k])
                                if k == 0 and sbi >= 2:
                                    v.wait_ge(s_t.h, mm_last[sbi - 2])
                                    v.wait_ge(s_wq.h, s_wq.v)
                                got = s_v.inc(v.tensor_tensor(
                                    w2s[sp][:, k * C:(k + 1) * C],
                                    sgb(ht)[:, 0:C], sgb(ht)[:, C:2 * C],
                                    ALU.add))
                                v.drain()
                                if ht in wq_sems:
                                    assert got == wq_sems[ht]
                                else:
                                    wq_sems[ht] = got

                    emit_adds(range(0, min(2, sbn)))

                    def late():
                        if sbn > 2:
                            emit_adds(range(2, sbn))
                        slab_ok[sbi][0] = wq_sems[base + sbn - 1]

                        @blk.sync
                        def _(sy):
                            for k in range(sbn):
                                ht = base + k
                                sy.wait_ge(s_v.h, wq_sems[ht])
                                s_wq.inc(sy.dma_start(
                                    out=dr["w2q_scr"][ht],
                                    in_=w2s[sp][:, k * C:(k + 1) * C]), 16)
                    prep_late[sbi] = late

                slab_ok[sbi] = [slab_v, slab_a, list(slab_w)]

            def emit_m2_units(sbi):
                sp = sbi % 2
                sbn = min(M2SB, NH - sbi * M2SB)
                slab_v, slab_a, slab_w = slab_ok[sbi]
                ps_cons = {}
                for tt in range(TPH):
                    for ch in range(2):
                        un = tt * 2 + ch
                        pu = un % 2

                        @blk.tensor
                        def _(t):
                            if un == 0:
                                if slab_v:
                                    t.wait_ge(s_v.h, slab_v)
                                if slab_a:
                                    t.wait_ge(s_a.h, slab_a)
                                for pp in range(2):
                                    if slab_w[pp]:
                                        t.wait_ge(s_w[pp].h, slab_w[pp])
                            if un >= 2:
                                t.wait_ge(s_v.h, ps_cons[un - 2])
                            elif sbi > 0:
                                t.wait_ge(s_v.h,
                                          prev_sb_cons[2 * TPH - 2 + un])
                            inst = None
                            for k in range(sbn):
                                for j in range(2):
                                    inst = t.matmul(
                                        ps_f[pu][:, j * 512:(j + 1) * 512],
                                        qhs[sp][:, k * HALF + tt * 128:
                                                k * HALF + (tt + 1) * 128],
                                        w2s[sp][:, k * C + ch * 1024 + j * 512:
                                                k * C + ch * 1024 +
                                                (j + 1) * 512],
                                        start=(k == 0), stop=(k == sbn - 1))
                            s_t.inc(inst)
                        mmk = s_t.v

                        @blk.vector
                        def _(v):
                            v.wait_ge(s_t.h, mmk)
                            dst = f_half[:, tt * C + ch * 1024:
                                         tt * C + (ch + 1) * 1024]
                            if sbi == 0:
                                s_v.inc(v.tensor_copy(dst, ps_f[pu][:]))
                            else:
                                s_v.inc(v.tensor_tensor(dst, dst, ps_f[pu][:],
                                                        ALU.add))
                            v.drain()
                        ps_cons[un] = s_v.v
                mm_last[sbi] = s_t.v
                prev_sb_cons.clear()
                prev_sb_cons.update(ps_cons)

            for sbi in range(NSB2 + 1):
                if sbi < NSB2:
                    emit_prep_early(sbi)
                if sbi >= 1:
                    emit_m2_units(sbi - 1)
                if sbi < NSB2 and prep_late[sbi] is not None:
                    prep_late[sbi]()
            hq_last[0] = hq_sems[NH - 2]
            hq_last[1] = hq_sems[NH - 1]
            f_ok = s_v.v
            fwait = dict(prev_sb_cons)

            ob_hist = {}
            st_hist = {}
            for tt in range(TPH):
                gt = half * TPH + tt
                fp = tt % 2
                xre = xre2[fp]
                gbuf = gbuf2[fp]

                @blk.sync
                def _(sy):
                    sy.wait_ge(s_v.h, ob_hist.get(tt - 2, fwait[2 * tt + 1]))
                    s_x.inc(sy.dma_start(
                        out=xre[:], in_=dr["x"][gt * 128:(gt + 1) * 128, :]),
                        16)
                xld = s_x.v

                @blk.tensor
                def _(t):
                    t.wait_ge(s_v.h, fwait[2 * tt + 1])
                    if tt >= 1:
                        t.wait_ge(s_v.h, ob_hist[tt - 1])
                    inst = None
                    for j in range(4):
                        inst = t.matmul(ps_op[:, j * 512:(j + 1) * 512],
                                        y_sb[:, gt * 128:(gt + 1) * 128],
                                        outprojT_sb[:, j * 512:(j + 1) * 512],
                                        start=True, stop=True)
                    s_t.inc(inst)
                opd = s_t.v

                @blk.scalar
                def _(sc):
                    sc.wait_ge(s_v.h, fwait[2 * tt + 1])
                    sc.wait_ge(s_o[fp].h, st_hist.get(tt - 2, prev_store[fp]))
                    s_a.inc(sc.activation(gbuf[:],
                                          f_half[:, tt * C:(tt + 1) * C],
                                          AF.Silu, scale=hscol[:, gt:gt + 1]))
                    sc.drain()
                gok = s_a.v

                @blk.vector
                def _(v):
                    v.wait_ge(s_x.h, xld)
                    v.wait_ge(s_t.h, opd)
                    v.wait_ge(s_a.h, gok)
                    v.tensor_scalar(fscr1[:], xre[:], m1c[:, gt:gt + 1],
                                    r1c[:, gt:gt + 1], ALU.subtract, ALU.mult)
                    v.drain()
                    v.tensor_tensor(fscr1[:], ps_op[:], fscr1[:], ALU.add)
                    v.drain()
                    v.tensor_tensor(fscr2[:], fscr1[:], gbuf[:], ALU.mult)
                    v.drain()
                    s_v.inc(v.tensor_tensor(gbuf[:], fscr2[:], xre[:],
                                            ALU.add))
                    v.drain()
                ob_hist[tt] = s_v.v

                @blk.sync
                def _(sy):
                    sy.wait_ge(s_v.h, ob_hist[tt])
                    s_o[fp].inc(sy.dma_start(
                        out=dr["out"][gt * 128:(gt + 1) * 128, :],
                        in_=gbuf[:]), 16)
                st_hist[tt] = s_o[fp].v
            prev_store = [st_hist[TPH - 2], st_hist[TPH - 1]]

    with nc.Block("fini") as blk:
        @blk.sync
        def _(sy):
            sy.wait_ge(s_o[0].h, s_o[0].v)
            sy.wait_ge(s_o[1].h, s_o[1].v)

    ps_es4.close()
    es_m2.close()
    es_mm.close()
    es.close()
    return nc


_PROG_CACHE = {}


def _get_prog():
    if "nc" not in _PROG_CACHE:
        _PROG_CACHE["nc"] = build_core_program()
    return _PROG_CACHE["nc"]


def kernel(L0_attn_feat_patch, L0_attn_ln_w, L0_attn_ln_b, L0_attn_in_proj_W,
           L0_attn_conv_w, L0_attn_conv_b, L0_attn_x_proj_W, L0_attn_dt_proj_W,
           L0_attn_dt_proj_b, L0_attn_A_log, L0_attn_D_param,
           L0_attn_out_proj_W, L0_ffn_w1, L0_ffn_w2, L0_ffn_w3):
    f32 = lambda a: np.ascontiguousarray(np.asarray(a), dtype=np.float32)
    x = f32(L0_attn_feat_patch)
    w1 = f32(L0_ffn_w1)
    w2 = f32(L0_ffn_w2)
    w3 = f32(L0_ffn_w3)

    def wtiles(w):
        t = w.T.reshape(NCC, 128, NH, 128).transpose(2, 1, 0, 3)
        return np.ascontiguousarray(t).reshape(NH, 128, NCC * 128)

    shared = {
        "w1L": wtiles(w1),
        "w3L": wtiles(w3),
        "w2T": np.ascontiguousarray(w2.T).reshape(NH, 128, C),
        "inprojT": np.ascontiguousarray(f32(L0_attn_in_proj_W).T),
        "outprojT": np.ascontiguousarray(f32(L0_attn_out_proj_W).T),
        "xprojT": np.ascontiguousarray(f32(L0_attn_x_proj_W).T),
        "dtT": np.ascontiguousarray(f32(L0_attn_dt_proj_W).T),
        "Alog": f32(L0_attn_A_log),
        "Dp": f32(L0_attn_D_param).reshape(D, 1),
        "convw": f32(L0_attn_conv_w).reshape(D, 3),
        "convb": f32(L0_attn_conv_b).reshape(D, 1),
        "dtb": f32(L0_attn_dt_proj_b).reshape(D, 1),
        "lnw": np.ascontiguousarray(f32(L0_attn_ln_w).reshape(NCC, 128).T),
        "lnb": np.ascontiguousarray(f32(L0_attn_ln_b).reshape(NCC, 128).T),
    }
    in_maps = []
    for i in range(8):
        m = dict(shared)
        m["x"] = np.ascontiguousarray(x[i])
        in_maps.append(m)

    nc = _get_prog()
    res = run_bass_kernel_spmd(nc, in_maps, core_ids=list(range(8)))
    return np.stack([np.asarray(r["out"], dtype=np.float32)
                     for r in res.results], axis=0)


# version 24
# speedup vs baseline: 1.0158x; 1.0158x over previous
import numpy as np
from contextlib import ExitStack

import concourse.bass as bass
import concourse.mybir as mybir
from concourse.bass_utils import run_bass_kernel_spmd

F32 = mybir.dt.float32
BF16 = mybir.dt.bfloat16
I32 = mybir.dt.int32
AF = mybir.ActivationFunctionType
ALU = mybir.AluOpType
AXX = mybir.AxisListType.X

L = 2048
C = 2048
H = 5632
D = 96
NST = 16
RK = 6
EPS = 1e-5
NCC = C // 128
NH = H // 128
NTT = L // 128
QT = L // 4
TPQ = 4
NE = 8
TPE = 2
ET = L // NE
HALF = L // 2
TPH = 8
WN = float(H * C)
M2S = [7, 7, 7, 7, 7, 7, 2]
M2MAX = 7


class Sem:
    def __init__(self, es, nc, name):
        self.h = es.enter_context(nc.semaphore(name))
        self.v = 0

    def inc(self, inst, n=1):
        inst.then_inc(self.h, n)
        self.v += n
        return self.v


def build_core_program(stop_after=None):
    nc = bass.Bass("TRN2", target_bir_lowering=False)
    dr = {}

    def din(name, shp, dt=F32, kind="ExternalInput"):
        dr[name] = nc.dram_tensor(name, list(shp), dt, kind=kind).ap()
        return dr[name]

    din("x", (L, C))
    din("w1L", (NH, 128, NCC * 128))
    din("w3L", (NH, 128, NCC * 128))
    din("w2T", (NH, 128, C))
    din("inprojT", (C, D))
    din("outprojT", (D, C))
    din("xprojT", (D, RK + 2 * NST))
    din("dtT", (RK, D))
    din("Alog", (D, NST))
    din("Dp", (D, 1))
    din("convw", (D, 3))
    din("convb", (D, 1))
    din("dtb", (D, 1))
    din("lnw", (128, NCC))
    din("lnb", (128, NCC))
    din("out", (L, C), kind="ExternalOutput")
    din("h_scr", (H, L), kind="Internal")
    din("w2q_scr", (NH, 128, C), BF16, kind="Internal")

    es = ExitStack()
    SBm = lambda n, shp, dt=F32: es.enter_context(nc.sbuf_tensor(n, list(shp), dt))
    MKSEM = lambda n: Sem(es, nc, n)

    ident = SBm("ident", (128, 128))
    ones1_128 = SBm("ones1_128", (1, 128))
    ones1_96 = SBm("ones1_96", (1, 96))
    onescol = SBm("onescol", (128, 1))
    A_sb = SBm("A_sb", (D, NST))
    Dp_sb = SBm("Dp_sb", (D, 1))
    convw_sb = SBm("convw_sb", (D, 3))
    convb_sb = SBm("convb_sb", (D, 1))
    dtb_sb = SBm("dtb_sb", (D, 1))
    lnw_sb = SBm("lnw_sb", (128, NCC))
    lnb_sb = SBm("lnb_sb", (128, NCC))
    outprojT_sb = SBm("outprojT_sb", (D, C))
    xprojT_sb = SBm("xprojT_sb", (D, RK + 2 * NST))
    dtT_sb = SBm("dtT_sb", (RK, D))
    m1c = SBm("m1c", (128, NTT))
    nm1c = SBm("nm1c", (128, NTT))
    r1c = SBm("r1c", (128, NTT))
    stat2 = SBm("stat2", (128, 3 * NTT))
    nm2c = SBm("nm2c", (128, NTT))
    xcol = SBm("xcol", (128, 4 * NTT))
    ssum = SBm("ssum", (128, NTT))
    sq12 = SBm("sq12", (128, 2 * NTT))
    sxinvB = SBm("sxinvB", (128, L))
    y_sb = SBm("y_sb", (D, L))
    mtmp = SBm("mtmp", (128, 4 * NH))
    thrall = SBm("thrall", (128, 9))
    srow = SBm("srow", (1, 16))
    mxc2 = SBm("mxc2", (128, NTT))
    hscol = SBm("hscol", (128, NTT))
    fscr1 = SBm("fscr1", (128, C))
    cns = SBm("cns", (128, 4))
    wstg = [SBm(f"wstg{i}", (128, NCC * 128)) for i in range(2)]
    lnwb = SBm("lnwb", (128, 2 * NCC))

    qint = fscr1[:].bitcast(I32)
    s_ld = MKSEM("s_ld")
    s_x = MKSEM("s_x")
    s_xb = MKSEM("s_xb")
    s_w = [MKSEM("s_w0"), MKSEM("s_w1")]
    s_hd = [MKSEM("s_hd0"), MKSEM("s_hd1")]
    s_wq = MKSEM("s_wq")
    s_w2 = [MKSEM("s_w20"), MKSEM("s_w21")]
    s_wm = [MKSEM("s_wm0"), MKSEM("s_wm1")]
    s_t = MKSEM("s_t")
    s_v = MKSEM("s_v")
    s_a = MKSEM("s_a")
    s_g = MKSEM("s_g")
    s_o = [MKSEM("s_o0"), MKSEM("s_o1")]
    s_fl = MKSEM("s_fl")

    es_qx = ExitStack()
    qx = es_qx.enter_context(nc.sbuf_tensor("qx", [128, NCC * L], BF16,
                                            side="right"))

    es_xin = ExitStack()
    xin = es_xin.enter_context(nc.sbuf_tensor("xin", [D, L], F32))
    wmst = [wstg[0], wstg[1]] + \
        [es_xin.enter_context(nc.sbuf_tensor(f"wmst{i}", [128, NCC * 128],
                                             F32)) for i in range(2)]
    s_wm4 = [MKSEM(f"s_wm4_{i}") for i in range(4)]

    es_ph = ExitStack()
    SBp = lambda n, shp, dt=F32: es_ph.enter_context(nc.sbuf_tensor(n, list(shp), dt))
    iob = SBp("iob", (128, 128))
    inprojT_sb = SBp("inprojT_sb", (128, NCC * D))
    wfold = SBp("wfold", (128, D))
    xh2 = [SBp(f"xh2_{i}", (128, TPE * C)) for i in range(2)]
    pscr = SBp("pscr", (128, C))
    xnTc = SBp("xnTc", (128, 4 * ET))
    rowbuf = SBp("rowbuf", (2 * TPE, 256))
    rows = SBp("rows", (1, 4 * ET))
    sxB = SBp("sxB", (128, ET))
    r2B = SBp("r2B", (D, ET))
    rm2B = SBp("rm2B", (D, ET))
    ipcorr = SBp("ipcorr", (D, ET))
    bvec = SBp("bvec", (D, 2))

    ps_es = ExitStack()
    PS1 = lambda n, shp: ps_es.enter_context(nc.psum_tensor(n, list(shp), F32))
    ps_tr = [PS1(f"ps_tr{i}", (128, ET)) for i in range(4)]
    ps_bc = [PS1(f"ps_bc{i}", (128, ET)) for i in range(2)]
    ps_ip2 = [PS1(f"ps_ip{i}", (D, ET)) for i in range(2)]
    ps_m = ps_bc[0]

    with nc.Block("setup") as blk:
        @blk.sync
        def _(sy):
            for src, dst in (("Alog", A_sb), ("Dp", Dp_sb), ("convw", convw_sb),
                             ("convb", convb_sb), ("dtb", dtb_sb),
                             ("lnw", lnw_sb), ("lnb", lnb_sb),
                             ("outprojT", outprojT_sb), ("xprojT", xprojT_sb),
                             ("dtT", dtT_sb)):
                s_ld.inc(sy.dma_start(out=dst[:], in_=dr[src][:]), 16)
            s_ld.inc(sy.dma_start(
                out=inprojT_sb[:].rearrange("p (a d) -> p a d", a=NCC),
                in_=dr["inprojT"].rearrange("(a p) d -> p a d", p=128)), 16)

        @blk.gpsimd
        def _(g):
            g.iota(iob[:], [[1, 128]], channel_multiplier=-1,
                   allow_small_or_imprecise_dtypes=True)
            g.drain()
            g.tensor_single_scalar(ident[:], iob[:], 0.0, ALU.is_equal)
            g.drain()
            g.memset(ones1_128[:], 1.0)
            g.memset(ones1_96[:], 1.0)
            g.memset(onescol[:], 1.0)
            g.memset(y_sb[:], 0.0)
            g.memset(cns[:, 0:1], EPS)
            g.memset(cns[:, 1:2], 1.0 / C)
            g.memset(cns[:, 2:3], 0.001)
            g.memset(cns[:, 3:4], 1.0)
            g.drain()
            s_g.inc(g.tensor_copy(srow[0:1, 15:16], cns[0:1, 0:1]))
            g.drain()

        @blk.vector
        def _(v):
            v.wait_ge(s_ld.h, s_ld.v)
            v.tensor_copy(lnwb[:].rearrange("p (a two) -> p a two", two=2)[:, :, 0:1],
                          lnb_sb[:].unsqueeze(2))
            v.drain()
            s_v.inc(v.tensor_copy(
                lnwb[:].rearrange("p (a two) -> p a two", two=2)[:, :, 1:2],
                lnw_sb[:].unsqueeze(2)))
            v.drain()

        @blk.scalar
        def _(sc):
            sc.wait_ge(s_ld.h, s_ld.v)
            sc.activation(A_sb[:], A_sb[:], AF.Exp)
            sc.drain()
            s_a.inc(sc.activation(A_sb[:], A_sb[:], AF.Copy, scale=-1.0))
            sc.drain()

        lnwb_ok = s_v.v

        @blk.tensor
        def _(t):
            t.wait_ge(s_ld.h, s_ld.v)
            t.wait_ge(s_v.h, lnwb_ok)
            inst = None
            for cc in range(NCC):
                inst = t.matmul(ps_m[0:D, 0:2],
                                inprojT_sb[:, cc * D:(cc + 1) * D],
                                lnwb[:, 2 * cc:2 * cc + 2],
                                start=(cc == 0), stop=(cc == NCC - 1))
            s_t.inc(inst)
        bvc = s_t.v

        @blk.vector
        def _(v):
            v.wait_ge(s_t.h, bvc)
            v.tensor_copy(bvec[:, 0:2], ps_m[0:D, 0:2])
            v.drain()
            for cc in range(NCC):
                v.tensor_single_scalar(inprojT_sb[:, cc * D:(cc + 1) * D],
                                       inprojT_sb[:, cc * D:(cc + 1) * D],
                                       lnw_sb[:, cc:cc + 1], ALU.mult)
                v.drain()
            s_v.inc(v.tensor_copy(srow[0:1, 14:15], bvec[0:1, 0:1]))
            v.drain()
        fold_ok = s_v.v

    tr_cons = {0: None, 1: None, 2: None, 3: None}

    mean_units = [("w1L", i, i) for i in range(NH)] + \
                 [("w3L", i, NH + i) for i in range(NH)]
    mean_st = {"next": 0, "act": {}, "max": {}}

    def emit_means(blk, count, eng="scalar", defer=False, pre_incs=0):
        j0 = mean_st["next"]
        j1 = min(j0 + count, len(mean_units))
        if j1 == j0:
            return None
        mean_st["next"] = j1
        wld = {}
        sem = s_a if eng == "scalar" else s_v
        for j in range(j0, j1):
            mean_st["act"][j] = (sem, sem.v + pre_incs + (j - j0 + 1))
        mean_st["max"][id(sem)] = (sem, sem.v + pre_incs + (j1 - j0))

        @blk.gpsimd
        def _(g):
            for j in range(j0, j1):
                wname, i, col = mean_units[j]
                par = j % 4
                if j >= 4:
                    psem, pval = mean_st["act"][j - 4]
                    g.wait_ge(psem.h, pval)
                s_wm4[par].inc(g.dma_start(out=wmst[par][:],
                                           in_=dr[wname][i]), 16)
                wld[j] = s_wm4[par].v

        def body_sc(sc):
            for j in range(j0, j1):
                wname, i, col = mean_units[j]
                par = j % 4
                sc.wait_ge(s_wm4[par].h, wld[j])
                s_a.inc(sc.activation(wmst[par][:], wmst[par][:], AF.Abs,
                                      accum_out=mtmp[:, col:col + 1]))
                sc.drain()

        def body_ve(v):
            for j in range(j0, j1):
                wname, i, col = mean_units[j]
                par = j % 4
                v.wait_ge(s_wm4[par].h, wld[j])
                s_v.inc(v.tensor_reduce(mtmp[:, col:col + 1],
                                        wmst[par][:], AXX, ALU.add,
                                        apply_absolute_value=True))
                v.drain()

        body = body_sc if eng == "scalar" else body_ve
        if defer:
            return body
        if eng == "scalar":
            blk.scalar(body_sc)
        else:
            blk.vector(body_ve)
        return None

    def finalize_mean(blk, cols, thr_base, psum_ap, extra_wait=None):
        c0, c1 = cols

        @blk.vector
        def _(v):
            if extra_wait is not None:
                v.wait_ge(s_a.h, extra_wait)
            else:
                for sem, val in mean_st["max"].values():
                    v.wait_ge(sem.h, val)
            s_v.inc(v.tensor_reduce(mtmp[:, c0:c0 + 1], mtmp[:, c0:c1], AXX,
                                    ALU.add))
            v.drain()
        red = s_v.v

        @blk.tensor
        def _(t):
            t.wait_ge(s_v.h, red)
            s_t.inc(t.matmul(psum_ap[0:1, 0:1], mtmp[:, c0:c0 + 1], onescol[:],
                             start=True, stop=True))
        tot = s_t.v

        @blk.vector
        def _(v):
            v.wait_ge(s_t.h, tot)
            v.tensor_single_scalar(srow[0:1, 0:1], psum_ap[0:1, 0:1],
                                   1.0 / WN, ALU.mult)
            v.drain()
            v.tensor_single_scalar(srow[0:1, 1:2], srow[0:1, 0:1], EPS,
                                   ALU.max)
            v.drain()
            v.tensor_single_scalar(srow[0:1, 2:3], srow[0:1, 1:2], 0.5,
                                   ALU.mult)
            v.drain()
            v.tensor_single_scalar(srow[0:1, 3:4], srow[0:1, 1:2], -0.5,
                                   ALU.mult)
            v.drain()
            v.reciprocal(srow[0:1, 4:5], srow[0:1, 1:2])
            v.drain()
            v.reciprocal(srow[0:1, 5:6], srow[0:1, 4:5])
            v.drain()
            s_v.inc(v.tensor_single_scalar(srow[0:1, 5:6], srow[0:1, 5:6],
                                           0.5, ALU.mult))
            v.drain()
        srk = s_v.v

        @blk.tensor
        def _(t):
            t.wait_ge(s_v.h, srk)
            s_t.inc(t.matmul(psum_ap[:, 0:6], ones1_128[:], srow[0:1, 0:6],
                             start=True, stop=True))
        bck = s_t.v

        @blk.vector
        def _(v):
            v.wait_ge(s_t.h, bck)
            v.tensor_copy(thrall[:, thr_base:thr_base + 1], psum_ap[:, 2:3])
            v.drain()
            v.tensor_copy(thrall[:, thr_base + 1:thr_base + 2],
                          psum_ap[:, 3:4])
            v.drain()
            s_v.inc(v.tensor_copy(thrall[:, thr_base + 2:thr_base + 3],
                                  psum_ap[:, 5:6]))
            v.drain()
        return s_v.v

    s_x2 = [s_x, s_xb]
    tp_last = {}
    p7_hist = {}
    evac_hist = {}
    ipd_hist = {}
    trs1_hist = {}
    trs2_hist = {}
    rdma1 = {}
    rdma2 = {}
    bc2c_hist = {}
    x_ok = {}
    tr_cons = {}

    with nc.Block("ph01") as blk:
        def emit_xdma(e):
            @blk.sync
            def _(sy):
                if e >= 2:
                    sy.wait_ge(s_t.h, tp_last[e - 2])
                    sy.wait_ge(s_a.h, p7_hist[e - 2])
                s_x2[e % 2].inc(sy.dma_start(
                    out=xh2[e % 2][:].rearrange("p (a c) -> p a c", a=TPE),
                    in_=dr["x"][e * ET:(e + 1) * ET, :]
                    .rearrange("(a p) c -> p a c", p=128)), 16)
                x_ok[e] = s_x2[e % 2].v

        p3_hist = {}
        p4v_hist = {}
        bc1c_hist = {}
        rb1_hist = {}
        g2s_hist = {}

        def emit_head(e):
            t0 = e * TPE
            xb = xh2[e % 2]
            sb1 = 3 * NTT + (e % 2) * 8

            @blk.scalar
            def _(sc):
                sc.wait_ge(s_x2[e % 2].h, x_ok[e])
                for tt in range(TPE):
                    sc.activation(pscr[:], xb[:, tt * C:(tt + 1) * C],
                                  AF.Copy, accum_out=ssum[:, tt:tt + 1])
                    sc.drain()
                sc.activation(m1c[:, t0:t0 + TPE], ssum[:, 0:TPE], AF.Copy,
                              scale=1.0 / C)
                sc.drain()
                sc.activation(nm1c[:, t0:t0 + TPE], ssum[:, 0:TPE],
                              AF.Copy, scale=-1.0 / C)
                sc.drain()
                for tt in range(TPE):
                    sc.activation(pscr[:], xb[:, tt * C:(tt + 1) * C],
                                  AF.Square,
                                  bias=nm1c[:, t0 + tt:t0 + tt + 1],
                                  accum_out=ssum[:, tt:tt + 1])
                    sc.drain()
                s_a.inc(sc.activation(sq12[:, t0:t0 + TPE], ssum[:, 0:TPE],
                                      AF.Sqrt, scale=cns[:, 1:2],
                                      bias=cns[:, 0:1]))
                sc.drain()
            p3_hist[e] = s_a.v

            @blk.vector
            def _(v):
                v.wait_ge(s_a.h, p3_hist[e])
                if e >= 2:
                    v.wait_ge(s_t.h, trs1_hist[e - 2])
                v.reciprocal(r1c[:, t0:t0 + TPE], sq12[:, t0:t0 + TPE])
                v.drain()
                for tt in range(TPE):
                    v.tensor_scalar(xb[:, tt * C:(tt + 1) * C],
                                    xb[:, tt * C:(tt + 1) * C],
                                    m1c[:, t0 + tt:t0 + tt + 1],
                                    r1c[:, t0 + tt:t0 + tt + 1],
                                    ALU.subtract, ALU.mult)
                    v.drain()
                for tt in range(TPE):
                    v.tensor_reduce(
                        stat2[:, 2 * NTT + t0 + tt:2 * NTT + t0 + tt + 1],
                        xb[:, tt * C:(tt + 1) * C], AXX, ALU.max,
                        apply_absolute_value=True)
                    v.drain()
                v.tensor_single_scalar(xcol[:, t0:t0 + TPE],
                                       stat2[:, 2 * NTT + t0:
                                             2 * NTT + t0 + TPE],
                                       EPS, ALU.max)
                v.drain()
                v.reciprocal(xcol[:, t0:t0 + TPE], xcol[:, t0:t0 + TPE])
                v.drain()
                v.tensor_single_scalar(xcol[:, t0:t0 + TPE],
                                       xcol[:, t0:t0 + TPE], 127.0, ALU.mult)
                v.drain()
                v.reciprocal(xcol[:, NTT + t0:NTT + t0 + TPE],
                             xcol[:, t0:t0 + TPE])
                v.drain()
                v.tensor_copy(xcol[:, sb1:sb1 + TPE], xcol[:, t0:t0 + TPE])
                v.drain()
                s_v.inc(v.tensor_copy(xcol[:, sb1 + TPE:sb1 + 2 * TPE],
                                      xcol[:, NTT + t0:NTT + t0 + TPE]))
                v.drain()
            p4v_hist[e] = s_v.v

            @blk.tensor
            def _(t):
                t.wait_ge(s_v.h, p4v_hist[e])
                if e >= 2:
                    t.wait_ge(s_v.h, bc2c_hist[e - 2])
                s_t.inc(t.transpose(ps_bc[0][0:2 * TPE, 0:128],
                                    xcol[:, sb1:sb1 + 2 * TPE], ident[:]))
            trs1_hist[e] = s_t.v

            @blk.vector
            def _(v):
                v.wait_ge(s_t.h, trs1_hist[e])
                if e >= 1:
                    v.wait_ge(s_ld.h, rdma1[e - 1])
                s_v.inc(v.tensor_copy(rowbuf[0:2 * TPE, 0:128],
                                      ps_bc[0][0:2 * TPE, 0:128]))
                v.drain()
            rb1_hist[e] = s_v.v

            @blk.sync
            def _(sy):
                sy.wait_ge(s_v.h, rb1_hist[e])
                for k in range(2):
                    s_ld.inc(sy.dma_start(
                        out=rows[0:1, k * ET:(k + 1) * ET],
                        in_=rowbuf[k * TPE:(k + 1) * TPE, 0:128]), 16)
                rdma1[e] = s_ld.v

            @blk.tensor
            def _(t):
                t.wait_ge(s_ld.h, rdma1[e])
                t.matmul(ps_bc[0][:, 0:ET], ones1_128[:], rows[0:1, 0:ET],
                         start=True, stop=True)
                s_t.inc(t.matmul(ps_bc[1][:, 0:ET], ones1_128[:],
                                 rows[0:1, ET:2 * ET], start=True, stop=True))
            bc1 = s_t.v

            @blk.vector
            def _(v):
                v.wait_ge(s_t.h, bc1)
                v.tensor_copy(sxB[:], ps_bc[0][:, 0:ET])
                v.drain()
                s_v.inc(v.tensor_copy(sxinvB[:, e * ET:(e + 1) * ET],
                                      ps_bc[1][:, 0:ET]))
                v.drain()
            bc1c_hist[e] = s_v.v

        def emit_chunks(e):
            t0 = e * TPE
            xb = xh2[e % 2]
            bc1c = bc1c_hist[e]

            @blk.scalar
            def _(sc):
                sc.wait_ge(s_v.h, p4v_hist[e])
                for tt in range(TPE):
                    sc.activation(pscr[:], xb[:, tt * C:(tt + 1) * C],
                                  AF.Copy, accum_out=ssum[:, tt:tt + 1])
                    sc.drain()
                sc.activation(stat2[:, t0:t0 + TPE], ssum[:, 0:TPE], AF.Copy,
                              scale=1.0 / C)
                sc.drain()
                sc.activation(nm2c[:, t0:t0 + TPE], ssum[:, 0:TPE], AF.Copy,
                              scale=-1.0 / C)
                sc.drain()
                for tt in range(TPE):
                    sc.activation(pscr[:], xb[:, tt * C:(tt + 1) * C],
                                  AF.Square,
                                  bias=nm2c[:, t0 + tt:t0 + tt + 1],
                                  accum_out=ssum[:, tt:tt + 1])
                    sc.drain()
                s_a.inc(sc.activation(sq12[:, NTT + t0:NTT + t0 + TPE],
                                      ssum[:, 0:TPE], AF.Sqrt,
                                      scale=cns[:, 1:2], bias=cns[:, 0:1]))
                sc.drain()
            p7_hist[e] = s_a.v

            ipmm = {}
            ps_ip = ps_ip2[e % 2]
            for cc in range(NCC):
                par = (2, 3, 0, 1)[cc % 4]
                xsl = xnTc[:, cc % 4 * ET:(cc % 4 + 1) * ET]

                @blk.tensor
                def _(t):
                    if tr_cons.get(par) is not None:
                        t.wait_ge(s_v.h, tr_cons[par])
                    if cc == 0:
                        t.wait_ge(s_v.h, p4v_hist[e])
                        if e >= 2:
                            t.wait_ge(s_v.h, evac_hist[e - 2])
                    inst = None
                    for tt in range(TPE):
                        inst = t.transpose(
                            ps_tr[par][:, tt * 128:(tt + 1) * 128],
                            xb[:, tt * C + cc * 128:tt * C + (cc + 1) * 128],
                            ident[:])
                    s_t.inc(inst)
                tp = s_t.v

                @blk.vector
                def _(v):
                    v.wait_ge(s_t.h, tp)
                    if cc >= 4:
                        v.wait_ge(s_t.h, ipmm[cc - 4])
                    s_v.inc(v.tensor_copy(xsl, ps_tr[par][:]))
                    v.drain()
                ev1 = s_v.v
                tr_cons[par] = ev1

                @blk.vector
                def _(v):
                    if cc == 0:
                        v.wait_ge(s_v.h, bc1c)
                    v.tensor_tensor(qint[:, 0:ET], xsl, sxB[:], ALU.mult)
                    v.drain()
                    s_v.inc(v.tensor_copy(
                        qx[:, cc * L + e * ET:cc * L + (e + 1) * ET],
                        qint[:, 0:ET]))
                    v.drain()

                @blk.tensor
                def _(t):
                    t.wait_ge(s_v.h, ev1)
                    inst = t.matmul(ps_ip[:],
                                    inprojT_sb[:, cc * D:(cc + 1) * D],
                                    xsl,
                                    start=(cc == 0), stop=(cc == NCC - 1))
                    s_t.inc(inst)
                ipmm[cc] = s_t.v
            tp_last[e] = ipmm[NCC - 1]
            ipd_hist[e] = ipmm[NCC - 1]

        def emit_tail(e):
            t0 = e * TPE
            sb2 = 3 * NTT + (e % 2) * 8 + 4
            ps_ip = ps_ip2[e % 2]
            psb_free = bc1c_hist[min(e + 1, NE - 1)]

            @blk.vector
            def _(v):
                v.wait_ge(s_a.h, p7_hist[e])
                if e >= 2:
                    v.wait_ge(s_t.h, trs2_hist[e - 2])
                v.reciprocal(stat2[:, NTT + t0:NTT + t0 + TPE],
                             sq12[:, NTT + t0:NTT + t0 + TPE])
                v.drain()
                v.tensor_tensor(xcol[:, 2 * NTT + t0:2 * NTT + t0 + TPE],
                                stat2[:, t0:t0 + TPE],
                                stat2[:, NTT + t0:NTT + t0 + TPE], ALU.mult)
                v.drain()
                v.tensor_copy(xcol[:, sb2:sb2 + TPE],
                              stat2[:, NTT + t0:NTT + t0 + TPE])
                v.drain()
                s_v.inc(v.tensor_copy(xcol[:, sb2 + TPE:sb2 + 2 * TPE],
                                      xcol[:, 2 * NTT + t0:
                                            2 * NTT + t0 + TPE]))
                v.drain()
            g2s_hist[e] = s_v.v

            @blk.tensor
            def _(t):
                t.wait_ge(s_v.h, g2s_hist[e])
                t.wait_ge(s_v.h, psb_free)
                s_t.inc(t.transpose(ps_bc[0][0:2 * TPE, 0:128],
                                    xcol[:, sb2:sb2 + 2 * TPE], ident[:]))
            trs2_hist[e] = s_t.v

            @blk.vector
            def _(v):
                v.wait_ge(s_t.h, trs2_hist[e])
                if e >= 1:
                    v.wait_ge(s_ld.h, rdma2[e - 1])
                s_v.inc(v.tensor_copy(rowbuf[0:2 * TPE, 128:256],
                                      ps_bc[0][0:2 * TPE, 0:128]))
                v.drain()
            rb2 = s_v.v

            @blk.sync
            def _(sy):
                sy.wait_ge(s_v.h, rb2)
                for k in range(2):
                    s_ld.inc(sy.dma_start(
                        out=rows[0:1, (2 + k) * ET:(3 + k) * ET],
                        in_=rowbuf[k * TPE:(k + 1) * TPE, 128:256]), 16)
                rdma2[e] = s_ld.v

            @blk.tensor
            def _(t):
                t.wait_ge(s_ld.h, rdma2[e])
                t.matmul(ps_bc[0][0:D, 0:ET], ones1_96[:],
                         rows[0:1, 2 * ET:3 * ET], start=True, stop=True)
                s_t.inc(t.matmul(ps_bc[1][0:D, 0:ET], ones1_96[:],
                                 rows[0:1, 3 * ET:4 * ET], start=True,
                                 stop=True))
            bc2 = s_t.v

            @blk.vector
            def _(v):
                v.wait_ge(s_t.h, bc2)
                v.tensor_copy(r2B[:], ps_bc[0][0:D, 0:ET])
                v.drain()
                s_v.inc(v.tensor_copy(rm2B[:], ps_bc[1][0:D, 0:ET]))
                v.drain()
            bc2c_hist[e] = s_v.v

            @blk.vector
            def _(v):
                v.wait_ge(s_t.h, ipd_hist[e])
                v.tensor_tensor(fscr1[0:D, 0:ET], ps_ip[:], r2B[:], ALU.mult)
                v.drain()
                v.tensor_scalar(ipcorr[:], rm2B[:], bvec[:, 1:2],
                                bvec[:, 0:1], ALU.mult, ALU.subtract)
                v.drain()
                s_v.inc(v.tensor_tensor(xin[:, e * ET:(e + 1) * ET],
                                        fscr1[0:D, 0:ET], ipcorr[:],
                                        ALU.subtract))
                v.drain()
            evac_hist[e] = s_v.v

        emit_xdma(0)
        emit_xdma(1)
        emit_head(0)
        for e in range(NE):
            if e >= 1:
                emit_tail(e - 1)
            emit_chunks(e)
            if e + 2 < NE:
                emit_xdma(e + 2)
            if e + 1 < NE:
                emit_head(e + 1)
            if e >= 1:
                emit_means(blk, 3, "scalar")
        emit_tail(NE - 1)

    ps_es.close()
    es_ph.close()
    xin_ok = evac_hist[NE - 1]
    if stop_after == "ph1":
        es_xin.close(); es_qx.close(); es.close()
        return nc

    es_sc = ExitStack()
    SBs = lambda n, shp, dt=F32: es_sc.enter_context(nc.sbuf_tensor(n, list(shp), dt))
    sel = SBs("sel", (NST, NST * D))
    ucv = SBs("ucv", (D, L))
    bp_sb = SBs("bp_sb", (NST, L))
    cp_sb = SBs("cp_sb", (NST, L))
    delta = SBs("delta", (D, L))
    du = SBs("du", (D, L))
    dAb = [SBs(f"dAb{i}", (D, L)) for i in range(2)] + [fscr1]
    dbc_dr = delta
    dA_n = dAb[0]
    dBu_n = ucv
    st_n = dBu_n
    yprod = xin

    ps_es2 = ExitStack()
    PS2 = lambda n, shp: ps_es2.enter_context(nc.psum_tensor(n, list(shp), F32))
    ps_sc = PS2("ps_sc", (D, L))
    ps_cc = PS2("ps_cc", (D, L))

    with nc.Block("attn") as blk:
        @blk.gpsimd
        def _(g):
            g.iota(sel[:], [[1, NST], [0, D]], channel_multiplier=-1,
                   allow_small_or_imprecise_dtypes=True)
            g.drain()
            s_g.inc(g.tensor_single_scalar(sel[:], sel[:], 0.0, ALU.is_equal))
            g.drain()
        sel_ok = s_g.v

        @blk.vector
        def _(v):
            v.wait_ge(s_v.h, xin_ok)
            s_v.inc(v.tensor_scalar(ucv[:], xin[:], convw_sb[:, 1:2],
                                    convb_sb[:], ALU.mult, ALU.add))
            v.drain()
        c1d = s_v.v

        @blk.vector
        def _(v):
            v.wait_ge(s_v.h, c1d)
            v.scalar_tensor_tensor(ucv[:, 1:L], xin[:, 0:L - 1],
                                   convw_sb[:, 0:1], ucv[:, 1:L],
                                   ALU.mult, ALU.add)
            v.drain()
            s_v.inc(v.scalar_tensor_tensor(ucv[:, 0:L - 1], xin[:, 1:L],
                                           convw_sb[:, 2:3], ucv[:, 0:L - 1],
                                           ALU.mult, ALU.add))
            v.drain()
        cvd = s_v.v

        @blk.scalar
        def _(sc):
            sc.wait_ge(s_v.h, cvd)
            s_a.inc(sc.activation(ucv[:], ucv[:], AF.Silu))
            sc.drain()
        ud = s_a.v
        emit_means(blk, 4, "scalar")

        xp_ev = {}
        for si, (c0, w, dst) in enumerate(((0, RK, dbc_dr),
                                           (RK, NST, bp_sb),
                                           (RK + NST, NST, cp_sb))):
            @blk.tensor
            def _(t):
                t.wait_ge(s_a.h, ud)
                if si > 0:
                    t.wait_ge(s_v.h, xp_ev[si - 1])
                inst = None
                for j in range(4):
                    inst = t.matmul(ps_sc[0:w, j * 512:(j + 1) * 512],
                                    xprojT_sb[:, c0:c0 + w],
                                    ucv[:, j * 512:(j + 1) * 512],
                                    start=True, stop=True)
                s_t.inc(inst)
            mmd = s_t.v

            @blk.vector
            def _(v):
                v.wait_ge(s_t.h, mmd)
                s_v.inc(v.tensor_copy(dst[0:w, :], ps_sc[0:w, :]))
                v.drain()
            xp_ev[si] = s_v.v

        @blk.tensor
        def _(t):
            t.wait_ge(s_v.h, xp_ev[2])
            inst = None
            for j in range(4):
                inst = t.matmul(ps_sc[0:D, j * 512:(j + 1) * 512], dtT_sb[:],
                                dbc_dr[0:RK, j * 512:(j + 1) * 512],
                                start=True, stop=True)
            s_t.inc(inst)
        dtd = s_t.v

        sps = dAb[1]

        @blk.vector
        def _(v):
            v.wait_ge(s_t.h, dtd)
            s_v.inc(v.tensor_single_scalar(sps[:], ps_sc[0:D, :], dtb_sb[:],
                                           ALU.add))
            v.drain()
        zd = s_v.v

        @blk.scalar
        def _(sc):
            sc.wait_ge(s_v.h, zd)
            sc.activation(yprod[:], sps[:], AF.Relu)
            sc.drain()
            sc.activation(dA_n[:], sps[:], AF.Abs)
            sc.drain()
            sc.activation(dA_n[:], dA_n[:], AF.Exp, scale=-1.0)
            sc.drain()
            s_a.inc(sc.activation(dA_n[:], dA_n[:], AF.Ln, bias=cns[0:D, 3:4]))
            sc.drain()
        spd = s_a.v

        @blk.vector
        def _(v):
            v.wait_ge(s_a.h, spd)
            s_v.inc(v.tensor_tensor(sps[:], yprod[:], dA_n[:], ALU.add))
            v.drain()
        sp2d = s_v.v

        @blk.scalar
        def _(sc):
            sc.wait_ge(s_v.h, sp2d)
            s_a.inc(sc.activation(delta[:], sps[:], AF.Copy,
                                  scale=cns[0:D, 2:3]))
            sc.drain()
        dltd = s_a.v

        @blk.vector
        def _(v):
            v.wait_ge(s_a.h, dltd)
            v.tensor_tensor(du[:], delta[:], ucv[:], ALU.mult)
            v.drain()
            s_v.inc(v.scalar_tensor_tensor(y_sb[:], ucv[:], Dp_sb[:], y_sb[:],
                                           ALU.mult, ALU.add))
            v.drain()
        dud = s_v.v

        dAd = {}

        def emit_exp(sc, n):
            s_a.inc(sc.activation(dAb[n % 3][0:D, :], delta[:], AF.Exp,
                                  scale=A_sb[:, n:n + 1]))
            sc.drain()
            dAd[n] = s_a.v

        @blk.scalar
        def _(sc):
            for n in range(3):
                emit_exp(sc, n)
        emit_means(blk, 8, "scalar")

        prev_dBu = None
        prev_yadd = None
        yadds = {}
        for n in range(NST):
            @blk.tensor
            def _(t):
                if n == 0:
                    t.wait_ge(s_v.h, dud)
                    t.wait_ge(s_g.h, sel_ok)
                else:
                    t.wait_ge(s_v.h, prev_dBu)
                inst = None
                for j in range(4):
                    inst = t.matmul(ps_sc[0:D, j * 512:(j + 1) * 512],
                                    sel[:, n * D:(n + 1) * D],
                                    bp_sb[:, j * 512:(j + 1) * 512],
                                    start=True, stop=True)
                s_t.inc(inst)
            bBd = s_t.v

            @blk.vector
            def _(v):
                v.wait_ge(s_t.h, bBd)
                s_v.inc(v.tensor_tensor(dBu_n[:], du[:], ps_sc[0:D, :],
                                        ALU.mult))
                v.drain()
            prev_dBu = s_v.v

            @blk.tensor
            def _(t):
                if prev_yadd is not None:
                    t.wait_ge(s_v.h, prev_yadd)
                inst = None
                for j in range(4):
                    inst = t.matmul(ps_cc[0:D, j * 512:(j + 1) * 512],
                                    sel[:, n * D:(n + 1) * D],
                                    cp_sb[:, j * 512:(j + 1) * 512],
                                    start=True, stop=True)
                s_t.inc(inst)
            bCd = s_t.v

            @blk.vector
            def _(v):
                v.wait_ge(s_a.h, dAd[n])
                v.tensor_tensor_scan(st_n[:], dAb[n % 3][0:D, :], dBu_n[:],
                                     0.0, ALU.mult, ALU.add)
                v.drain()
                v.wait_ge(s_t.h, bCd)
                v.tensor_tensor(yprod[:], st_n[:], ps_cc[0:D, :], ALU.mult)
                v.drain()
                s_v.inc(v.tensor_tensor(y_sb[:], y_sb[:], yprod[:], ALU.add))
                v.drain()
            prev_yadd = s_v.v
            yadds[n] = s_v.v

            mbody = emit_means(blk, 4, "scalar", defer=True,
                               pre_incs=(1 if n + 3 < NST else 0))
            if n + 3 < NST or mbody is not None:
                @blk.scalar
                def _(sc):
                    sc.wait_ge(s_v.h, yadds[n])
                    if n + 3 < NST:
                        emit_exp(sc, n + 3)
                    if mbody is not None:
                        mbody(sc)

        emit_means(blk, len(mean_units) - mean_st["next"])

    ps_es2.close()
    es_sc.close()
    es_xin.close()
    if stop_after == "scan":
        es_qx.close(); es.close()
        return nc

    es_mm = ExitStack()
    SBmm = lambda n, shp, dt=F32: es_mm.enter_context(nc.sbuf_tensor(n, list(shp), dt))
    fscr2 = SBmm("fscr2", (128, C))
    hbuf = [SBmm(f"hbuf{i}", (128, HALF)) for i in range(2)]
    wstg2 = [SBmm(f"wstg2{i}", (128, NCC * 128)) for i in range(2)]
    shB = SBmm("shB", (128, L))

    if stop_after == "means":
        es_mm.close(); es_qx.close(); es.close()
        return nc

    es_m13 = ExitStack()
    w1q = [es_m13.enter_context(nc.sbuf_tensor(f"w1q{i}", [128, NCC * 128], BF16))
           for i in range(2)]
    w2m = [es_m13.enter_context(nc.sbuf_tensor(f"w2m{i}", [128, HALF], F32))
           for i in range(2)]
    gscr = es_m13.enter_context(nc.sbuf_tensor("gscr", [128, NCC * 128], BF16))
    gscr2 = es_m13.enter_context(nc.sbuf_tensor("gscr2", [128, NCC * 128], BF16))
    gscr3 = es_m13.enter_context(nc.sbuf_tensor("gscr3", [128, NCC * 128], BF16))
    gscr4 = es_m13.enter_context(nc.sbuf_tensor("gscr4", [128, NCC * 128], BF16))
    w3q = [es_m13.enter_context(nc.sbuf_tensor(f"w3q{i}", [128, NCC * 128], BF16))
           for i in range(2)]
    hmax = es_m13.enter_context(nc.sbuf_tensor("hmax", [128, L], F32))

    ps_es3 = ExitStack()
    PS3 = lambda n, shp: ps_es3.enter_context(nc.psum_tensor(n, list(shp), F32))
    ps_a = [PS3(f"ps_a{i}", (128, HALF)) for i in range(2)]
    ps_b = [PS3(f"ps_b{i}", (128, HALF)) for i in range(2)]

    with nc.Block("m13") as blk:
        @blk.gpsimd
        def _(g):
            s_g.inc(g.memset(hmax[:], 0.0))
            g.drain()
        hmz = s_g.v

        thr13 = finalize_mean(blk, (0, NH), 0, ps_a[0])
        thr13 = finalize_mean(blk, (NH, 2 * NH), 3, ps_a[0])

        quant_done = {}
        quant_vec = {}
        evac_done = {}
        hbuf_free = {}
        mm_done_per_ht = {}
        w2m_act = {}
        w2m_sems = {}
        w3add_hist = {}
        w1add_hist = {}

        def emit_quant(ht):
            par = ht % 2

            @blk.sync
            def _(sy):
                if ht >= 2:
                    sy.wait_ge(s_v.h, quant_done[ht - 2])
                s_w[par].inc(sy.dma_start(out=wstg[par][:],
                                          in_=dr["w1L"][ht]), 16)
                s_w2[par].inc(sy.dma_start(out=wstg2[par][:],
                                           in_=dr["w3L"][ht]), 16)
            w1ld = s_w[par].v
            w3ld = s_w2[par].v

            @blk.scalar
            def _(sc):
                sc.wait_ge(s_w[par].h, w1ld)
                if ht < 2:
                    sc.wait_ge(s_v.h, thr13)
                if ht >= 1:
                    sc.wait_ge(s_v.h, w1add_hist[ht - 1])
                sc.activation(gscr[:], wstg[par][:], AF.Sign,
                              bias=thrall[:, 1:2])
                sc.drain()
                s_a.inc(sc.activation(gscr2[:], wstg[par][:], AF.Sign,
                                      bias=thrall[:, 0:1]))
                sc.drain()
                if ht >= 1:
                    sc.wait_ge(s_v.h, w3add_hist[ht - 1])
                sc.wait_ge(s_w2[par].h, w3ld)
                sc.activation(gscr3[:], wstg2[par][:], AF.Sign,
                              bias=thrall[:, 4:5])
                sc.drain()
                s_a.inc(sc.activation(gscr4[:], wstg2[par][:], AF.Sign,
                                      bias=thrall[:, 3:4]))
                sc.drain()
            w3sg = s_a.v
            w1sg = w3sg - 1

            @blk.vector
            def _(v):
                v.wait_ge(s_a.h, w1sg)
                if ht >= 2:
                    v.wait_ge(s_t.h, mm_done_per_ht[ht - 2])
                s_v.inc(v.tensor_tensor(w1q[par][:], gscr[:], gscr2[:],
                                        ALU.add))
                v.drain()
            w1add = s_v.v
            w1add_hist[ht] = s_v.v

            @blk.vector
            def _(v):
                v.wait_ge(s_a.h, w3sg)
                s_v.inc(v.tensor_tensor(w3q[par][:], gscr3[:], gscr4[:],
                                        ALU.add))
                v.drain()
            quant_done[ht] = s_v.v
            quant_vec[ht] = s_v.v
            w3add_hist[ht] = s_v.v

        def emit_units(ht):
            par = ht % 2
            for th in range(2):
                u = 2 * ht + th
                pu = u % 2

                @blk.tensor
                def _(t):
                    t.wait_ge(s_v.h, quant_done[ht])
                    if u >= 2:
                        t.wait_ge(s_v.h, evac_done[u - 2])
                    for cc in range(NCC):
                        for j in range(2):
                            t.matmul(ps_a[pu][:, j * 512:(j + 1) * 512],
                                     w1q[par][:, cc * 128:(cc + 1) * 128],
                                     qx[:, cc * L + th * HALF + j * 512:
                                        cc * L + th * HALF + (j + 1) * 512],
                                     start=(cc == 0), stop=(cc == NCC - 1))
                    inst = None
                    for cc in range(NCC):
                        for j in range(2):
                            inst = t.matmul(
                                ps_b[pu][:, j * 512:(j + 1) * 512],
                                w3q[par][:, cc * 128:(cc + 1) * 128],
                                qx[:, cc * L + th * HALF + j * 512:
                                   cc * L + th * HALF + (j + 1) * 512],
                                start=(cc == 0), stop=(cc == NCC - 1))
                    s_t.inc(inst)
                mmd = s_t.v
                mm_done_per_ht[ht] = mmd

                @blk.vector
                def _(v):
                    v.wait_ge(s_t.h, mmd)
                    if pu in hbuf_free:
                        v.wait_ge(s_hd[pu].h, hbuf_free[pu][0])
                        v.wait_ge(s_v.h, hbuf_free[pu][1])
                    s_v.inc(v.tensor_tensor(
                        fscr1[:, 0:HALF], ps_a[pu][:],
                        sxinvB[:, th * HALF:(th + 1) * HALF], ALU.mult))
                    v.drain()
                ta_ok = s_v.v

                @blk.scalar
                def _(sc):
                    sc.wait_ge(s_v.h, ta_ok)
                    s_a.inc(sc.activation(fscr2[:, 0:HALF], fscr1[:, 0:HALF],
                                          AF.Silu, scale=thrall[:, 2:3]))
                    sc.drain()
                sld = s_a.v

                @blk.vector
                def _(v):
                    v.wait_ge(s_a.h, sld)
                    v.tensor_tensor(fscr1[:, 0:HALF], ps_b[pu][:],
                                    sxinvB[:, th * HALF:(th + 1) * HALF],
                                    ALU.mult)
                    v.drain()
                    s_v.inc(v.scalar_tensor_tensor(
                        hbuf[pu][:], fscr1[:, 0:HALF], thrall[:, 5:6],
                        fscr2[:, 0:HALF], ALU.mult, ALU.mult))
                    v.drain()
                evac_done[u] = s_v.v
                h_ok = s_v.v

                @blk.gpsimd
                def _(g):
                    g.wait_ge(s_v.h, h_ok)
                    s_hd[pu].inc(g.dma_start(
                        out=dr["h_scr"][ht * 128:(ht + 1) * 128,
                                        th * HALF:(th + 1) * HALF],
                        in_=hbuf[pu][:]), 16)

                @blk.vector
                def _(v):
                    if u == 0:
                        v.wait_ge(s_g.h, hmz)
                    v.scalar_tensor_tensor(fscr1[:, 0:HALF], hbuf[pu][:],
                                           -1.0, hbuf[pu][:],
                                           ALU.mult, ALU.max)
                    v.drain()
                    s_v.inc(v.tensor_tensor(
                        hmax[:, th * HALF:(th + 1) * HALF],
                        hmax[:, th * HALF:(th + 1) * HALF],
                        fscr1[:, 0:HALF], ALU.max))
                    v.drain()
                hbuf_free[pu] = (s_hd[pu].v, s_v.v)

        def emit_w2mean(ht):
            for hf in range(2):
                j = 2 * ht + hf
                wpar = j % 2

                @blk.sync
                def _(sy):
                    if j >= 2:
                        sy.wait_ge(s_a.h, w2m_act[j - 2])
                    s_wm[wpar].inc(sy.dma_start(
                        out=w2m[wpar][:],
                        in_=dr["w2T"][ht][:, hf * HALF:(hf + 1) * HALF]), 16)
                w2m_sems[j] = s_wm[wpar].v

                @blk.scalar
                def _(sc):
                    sc.wait_ge(s_wm[wpar].h, w2m_sems[j])
                    s_a.inc(sc.activation(
                        w2m[wpar][:], w2m[wpar][:], AF.Abs,
                        accum_out=mtmp[:, (2 + hf) * NH + ht:
                                       (2 + hf) * NH + ht + 1]))
                    sc.drain()
                w2m_act[j] = s_a.v

        for ht in range(NH + 1):
            if ht < NH:
                emit_quant(ht)
            if ht >= 1:
                emit_units(ht - 1)
            if ht < NH:
                emit_w2mean(ht)

    hstore_total = [s_hd[0].v, s_hd[1].v]
    es_qx.close()
    if stop_after == "m13":
        ps_es3.close(); es_m13.close(); es_mm.close(); es.close()
        return nc

    ps_es3.close()

    es_hf = ExitStack()
    rowb2 = es_hf.enter_context(nc.sbuf_tensor("rowb2", [NTT, 128], F32))
    rows2 = es_hf.enter_context(nc.sbuf_tensor("rows2", [1, 2 * L], F32))
    ps_es4 = ExitStack()
    PS4 = lambda n, shp: ps_es4.enter_context(nc.psum_tensor(n, list(shp), F32))
    ps_op = PS4("ps_op", (128, C))
    ps_f = [PS4(f"ps_f{i}", (128, HALF)) for i in range(2)]

    with nc.Block("hfin") as blk:
        thrw2 = finalize_mean(blk, (2 * NH, 4 * NH), 6, ps_op,
                              extra_wait=w2m_act[2 * NH - 1])

        @blk.tensor
        def _(t):
            t.wait_ge(s_v.h, thrw2)
            inst = None
            for j in range(NTT):
                inst = t.transpose(ps_op[:, j * 128:(j + 1) * 128],
                                   hmax[:, j * 128:(j + 1) * 128], ident[:])
            s_t.inc(inst)
        hts = s_t.v

        @blk.vector
        def _(v):
            v.wait_ge(s_t.h, hts)
            for j in range(NTT):
                v.tensor_reduce(mxc2[:, j:j + 1],
                                ps_op[:, j * 128:(j + 1) * 128], AXX, ALU.max)
                v.drain()
            v.tensor_single_scalar(hscol[:], mxc2[:], EPS, ALU.max)
            v.drain()
            v.reciprocal(hscol[:], hscol[:])
            v.drain()
            v.tensor_single_scalar(hscol[:], hscol[:], 127.0, ALU.mult)
            v.drain()
            v.reciprocal(hscol[:], hscol[:])
            v.drain()
            s_v.inc(v.tensor_single_scalar(hscol[:], hscol[:],
                                           thrall[:, 8:9], ALU.mult))
            v.drain()
        hcol_ok = s_v.v

        @blk.tensor
        def _(t):
            t.wait_ge(s_v.h, hcol_ok)
            s_t.inc(t.transpose(ps_op[0:NTT, 0:128], mxc2[:], ident[:]))
        ht2 = s_t.v

        @blk.vector
        def _(v):
            v.wait_ge(s_t.h, ht2)
            s_v.inc(v.tensor_copy(rowb2[:], ps_op[0:NTT, 0:128]))
            v.drain()
        rb2 = s_v.v

        @blk.gpsimd
        def _(g):
            g.wait_ge(s_v.h, rb2)
            s_fl.inc(g.dma_start(out=rows2[0:1, 0:L], in_=rowb2[:]), 16)
        r2ok = s_fl.v

        @blk.vector
        def _(v):
            v.wait_ge(s_fl.h, r2ok)
            v.tensor_single_scalar(rows2[0:1, L:2 * L], rows2[0:1, 0:L], EPS,
                                   ALU.max)
            v.drain()
            v.reciprocal(rows2[0:1, L:2 * L], rows2[0:1, L:2 * L])
            v.drain()
            s_v.inc(v.tensor_single_scalar(rows2[0:1, L:2 * L],
                                           rows2[0:1, L:2 * L], 127.0,
                                           ALU.mult))
            v.drain()
        shrow_ok = s_v.v

        @blk.tensor
        def _(t):
            t.wait_ge(s_v.h, shrow_ok)
            inst = None
            for j in range(4):
                inst = t.matmul(ps_op[:, j * 512:(j + 1) * 512], ones1_128[:],
                                rows2[0:1, L + j * 512:L + (j + 1) * 512],
                                start=True, stop=True)
            s_t.inc(inst)
        shb = s_t.v

        @blk.vector
        def _(v):
            v.wait_ge(s_t.h, shb)
            s_v.inc(v.tensor_copy(shB[:], ps_op[:]))
            v.drain()
    shB_ok = s_v.v
    es_hf.close()
    es_m13.close()

    M2SB = 4
    NSB2 = 11
    es_m2 = ExitStack()
    f_half = es_m2.enter_context(nc.sbuf_tensor("f_half", [128, TPH * C], F32))
    w2s = [es_m2.enter_context(nc.sbuf_tensor(f"w2s{i}", [128, M2SB * C], BF16))
           for i in range(2)]
    qhs = [es_m2.enter_context(nc.sbuf_tensor(f"qhs{i}", [128, M2SB * HALF],
                                              BF16)) for i in range(2)]
    w4stg = [wstg[0], wstg[1], sxinvB]
    s_wx = [MKSEM("s_wx0")]
    s_w4s = [s_w[0], s_w[1], s_wx[0]]

    xre2 = [wstg2[0], wstg[0]]
    gbuf2 = [wstg2[1], wstg[1]]

    wq_sems = {}
    hq_last = {0: None, 1: None}
    prev_store = [0, 0]
    for half in range(2):
        hq_sems = {}
        hm1_sems = {}
        with nc.Block(f"m2_{half}") as blk:
            mm_last = {}
            slab_ok = {}
            prev_sb_cons = {}

            prep_late = {}

            def emit_prep_early(sbi):
                sp = sbi % 2
                sbn = min(M2SB, NH - sbi * M2SB)
                base = sbi * M2SB
                sgb = lambda ht: (fscr2[:].bitcast(BF16) if ht % 2 == 0
                                  else wstg2[0][:].bitcast(BF16))
                slab_v = 0
                slab_w = [0, 0]

                nsg = sbn if half == 0 else 0
                a0 = s_a.v
                v0h = s_v.v
                for k in range(sbn):
                    if k < 2:
                        hq_sems[base + k] = a0 + k + 1
                    else:
                        hq_sems[base + k] = a0 + 2 + nsg + (k - 1)
                    hm1_sems[base + k] = v0h + k + 1
                if half == 0:
                    for k in range(min(2, sbn)):
                        wq_sems[base + k] = v0h + sbn + k + 1

                hlds = {}

                @blk.gpsimd
                def _(g):
                    for k in range(sbn):
                        ht = base + k
                        par = ht % 2
                        if ht >= 2:
                            g.wait_ge(s_v.h, hm1_sems[ht - 2])
                        elif half == 0:
                            g.wait_ge(s_hd[par].h, hstore_total[par])
                        else:
                            g.wait_ge(s_a.h, hq_last[par])
                        s_hd[par].inc(g.dma_start(
                            out=hbuf[par][:],
                            in_=dr["h_scr"][ht * 128:(ht + 1) * 128,
                                            half * HALF:(half + 1) * HALF]),
                            16)
                        hlds[k] = s_hd[par].v

                wlds = {}
                if half == 0:
                    @blk.sync
                    def _(sy):
                        for k in range(sbn):
                            ht = base + k
                            p4 = ht % 3
                            if ht >= 3:
                                sy.wait_ge(s_v.h, wq_sems[ht - 3])
                            s_w4s[p4].inc(sy.dma_start(
                                out=w4stg[p4][:], in_=dr["w2T"][ht]), 16)
                            wlds[k] = s_w4s[p4].v
                else:
                    @blk.sync
                    def _(sy):
                        for k in range(sbn):
                            ht = base + k
                            if sbi == 0 and k == 0:
                                sy.wait_ge(s_wq.h, s_wq.v)
                            if k == 0 and sbi >= 2:
                                sy.wait_ge(s_t.h, mm_last[sbi - 2])
                            s_w[sp].inc(sy.dma_start(
                                out=w2s[sp][:, k * C:(k + 1) * C],
                                in_=dr["w2q_scr"][ht]), 16)
                    slab_w[sp] = s_w[sp].v
                    prep_late[sbi] = None

                hm1s = {}

                @blk.vector
                def _(v):
                    for k in range(sbn):
                        ht = base + k
                        par = ht % 2
                        qv = (qint[:, 0:HALF] if ht % 2 == 0
                              else qint[:, HALF:2 * HALF])
                        v.wait_ge(s_hd[par].h, hlds[k])
                        if ht >= 2:
                            v.wait_ge(s_a.h, hq_sems[ht - 2])
                        if half == 0 and ht == 0:
                            v.wait_ge(s_v.h, shB_ok)
                        got = s_v.inc(v.tensor_tensor(
                            qv, hbuf[par][:],
                            shB[:, half * HALF:(half + 1) * HALF],
                            ALU.mult))
                        v.drain()
                        assert got == hm1_sems[ht]
                        hm1s[k] = got

                def emit_copies(kr):
                    @blk.scalar
                    def _(sc):
                        for k in kr:
                            ht = base + k
                            qv = (qint[:, 0:HALF] if ht % 2 == 0
                                  else qint[:, HALF:2 * HALF])
                            sc.wait_ge(s_v.h, hm1s[k])
                            if k == 0 and sbi >= 2:
                                sc.wait_ge(s_t.h, mm_last[sbi - 2])
                            got = s_a.inc(sc.activation(
                                qhs[sp][:, k * HALF:(k + 1) * HALF], qv,
                                AF.Copy))
                            sc.drain()
                            assert got == hq_sems[ht]

                emit_copies(range(0, min(2, sbn)))

                if half == 0:
                    sgs = {}

                    @blk.scalar
                    def _(sc):
                        for k in range(sbn):
                            ht = base + k
                            p4 = ht % 3
                            sc.wait_ge(s_w4s[p4].h, wlds[k])
                            if ht >= 2 and (ht - 2) in wq_sems:
                                sc.wait_ge(s_v.h, wq_sems[ht - 2])
                            sc.activation(sgb(ht)[:, 0:C], w4stg[p4][:],
                                          AF.Sign, bias=thrall[:, 7:8])
                            sc.drain()
                            got = s_a.inc(sc.activation(
                                sgb(ht)[:, C:2 * C], w4stg[p4][:], AF.Sign,
                                bias=thrall[:, 6:7]))
                            sc.drain()
                            sgs[k] = got

                if sbn > 2:
                    emit_copies(range(2, sbn))
                slab_a = hq_sems[base + sbn - 1]

                if half == 0:
                    for k in range(min(2, sbn)):
                        assert wq_sems[base + k] == s_v.v + k + 1

                    def emit_adds(kr):
                        @blk.vector
                        def _(v):
                            for k in kr:
                                ht = base + k
                                v.wait_ge(s_a.h, sgs[k])
                                if k == 0 and sbi >= 2:
                                    v.wait_ge(s_t.h, mm_last[sbi - 2])
                                    v.wait_ge(s_wq.h, s_wq.v)
                                got = s_v.inc(v.tensor_tensor(
                                    w2s[sp][:, k * C:(k + 1) * C],
                                    sgb(ht)[:, 0:C], sgb(ht)[:, C:2 * C],
                                    ALU.add))
                                v.drain()
                                if ht in wq_sems:
                                    assert got == wq_sems[ht]
                                else:
                                    wq_sems[ht] = got

                    emit_adds(range(0, min(2, sbn)))

                    def late():
                        if sbn > 2:
                            emit_adds(range(2, sbn))
                        slab_ok[sbi][0] = wq_sems[base + sbn - 1]

                        @blk.sync
                        def _(sy):
                            for k in range(sbn):
                                ht = base + k
                                sy.wait_ge(s_v.h, wq_sems[ht])
                                s_wq.inc(sy.dma_start(
                                    out=dr["w2q_scr"][ht],
                                    in_=w2s[sp][:, k * C:(k + 1) * C]), 16)
                    prep_late[sbi] = late

                slab_ok[sbi] = [slab_v, slab_a, list(slab_w)]

            def emit_m2_units(sbi):
                sp = sbi % 2
                sbn = min(M2SB, NH - sbi * M2SB)
                slab_v, slab_a, slab_w = slab_ok[sbi]
                ps_cons = {}
                for tt in range(TPH):
                    for ch in range(2):
                        un = tt * 2 + ch
                        pu = un % 2

                        @blk.tensor
                        def _(t):
                            if un == 0:
                                if slab_v:
                                    t.wait_ge(s_v.h, slab_v)
                                if slab_a:
                                    t.wait_ge(s_a.h, slab_a)
                                for pp in range(2):
                                    if slab_w[pp]:
                                        t.wait_ge(s_w[pp].h, slab_w[pp])
                            if un >= 2:
                                t.wait_ge(s_v.h, ps_cons[un - 2])
                            elif sbi > 0:
                                t.wait_ge(s_v.h,
                                          prev_sb_cons[2 * TPH - 2 + un])
                            inst = None
                            for k in range(sbn):
                                for j in range(2):
                                    inst = t.matmul(
                                        ps_f[pu][:, j * 512:(j + 1) * 512],
                                        qhs[sp][:, k * HALF + tt * 128:
                                                k * HALF + (tt + 1) * 128],
                                        w2s[sp][:, k * C + ch * 1024 + j * 512:
                                                k * C + ch * 1024 +
                                                (j + 1) * 512],
                                        start=(k == 0), stop=(k == sbn - 1))
                            s_t.inc(inst)
                        mmk = s_t.v

                        @blk.vector
                        def _(v):
                            v.wait_ge(s_t.h, mmk)
                            dst = f_half[:, tt * C + ch * 1024:
                                         tt * C + (ch + 1) * 1024]
                            if sbi == 0:
                                s_v.inc(v.tensor_copy(dst, ps_f[pu][:]))
                            else:
                                s_v.inc(v.tensor_tensor(dst, dst, ps_f[pu][:],
                                                        ALU.add))
                            v.drain()
                        ps_cons[un] = s_v.v
                mm_last[sbi] = s_t.v
                prev_sb_cons.clear()
                prev_sb_cons.update(ps_cons)

            for sbi in range(NSB2 + 1):
                if sbi < NSB2:
                    emit_prep_early(sbi)
                if sbi >= 1:
                    emit_m2_units(sbi - 1)
                if sbi < NSB2 and prep_late[sbi] is not None:
                    prep_late[sbi]()
            hq_last[0] = hq_sems[NH - 2]
            hq_last[1] = hq_sems[NH - 1]
            f_ok = s_v.v
            fwait = dict(prev_sb_cons)

            ob_hist = {}
            st_hist = {}
            for tt in range(TPH):
                gt = half * TPH + tt
                fp = tt % 2
                xre = xre2[fp]
                gbuf = gbuf2[fp]

                @blk.sync
                def _(sy):
                    sy.wait_ge(s_v.h, ob_hist.get(tt - 2, fwait[2 * tt + 1]))
                    s_x.inc(sy.dma_start(
                        out=xre[:], in_=dr["x"][gt * 128:(gt + 1) * 128, :]),
                        16)
                xld = s_x.v

                @blk.tensor
                def _(t):
                    t.wait_ge(s_v.h, fwait[2 * tt + 1])
                    if tt >= 1:
                        t.wait_ge(s_v.h, ob_hist[tt - 1])
                    inst = None
                    for j in range(4):
                        inst = t.matmul(ps_op[:, j * 512:(j + 1) * 512],
                                        y_sb[:, gt * 128:(gt + 1) * 128],
                                        outprojT_sb[:, j * 512:(j + 1) * 512],
                                        start=True, stop=True)
                    s_t.inc(inst)
                opd = s_t.v

                @blk.scalar
                def _(sc):
                    sc.wait_ge(s_v.h, fwait[2 * tt + 1])
                    sc.wait_ge(s_o[fp].h, st_hist.get(tt - 2, prev_store[fp]))
                    s_a.inc(sc.activation(gbuf[:],
                                          f_half[:, tt * C:(tt + 1) * C],
                                          AF.Silu, scale=hscol[:, gt:gt + 1]))
                    sc.drain()
                gok = s_a.v

                @blk.vector
                def _(v):
                    v.wait_ge(s_x.h, xld)
                    v.wait_ge(s_t.h, opd)
                    v.wait_ge(s_a.h, gok)
                    v.tensor_scalar(fscr1[:], xre[:], m1c[:, gt:gt + 1],
                                    r1c[:, gt:gt + 1], ALU.subtract, ALU.mult)
                    v.drain()
                    v.tensor_tensor(fscr1[:], ps_op[:], fscr1[:], ALU.add)
                    v.drain()
                    v.tensor_tensor(fscr2[:], fscr1[:], gbuf[:], ALU.mult)
                    v.drain()
                    s_v.inc(v.tensor_tensor(gbuf[:], fscr2[:], xre[:],
                                            ALU.add))
                    v.drain()
                ob_hist[tt] = s_v.v

                @blk.sync
                def _(sy):
                    sy.wait_ge(s_v.h, ob_hist[tt])
                    s_o[fp].inc(sy.dma_start(
                        out=dr["out"][gt * 128:(gt + 1) * 128, :],
                        in_=gbuf[:]), 16)
                st_hist[tt] = s_o[fp].v
            prev_store = [st_hist[TPH - 2], st_hist[TPH - 1]]

    with nc.Block("fini") as blk:
        @blk.sync
        def _(sy):
            sy.wait_ge(s_o[0].h, s_o[0].v)
            sy.wait_ge(s_o[1].h, s_o[1].v)

    ps_es4.close()
    es_m2.close()
    es_mm.close()
    es.close()
    return nc


_PROG_CACHE = {}


def _get_prog():
    if "nc" not in _PROG_CACHE:
        _PROG_CACHE["nc"] = build_core_program()
    return _PROG_CACHE["nc"]


def kernel(L0_attn_feat_patch, L0_attn_ln_w, L0_attn_ln_b, L0_attn_in_proj_W,
           L0_attn_conv_w, L0_attn_conv_b, L0_attn_x_proj_W, L0_attn_dt_proj_W,
           L0_attn_dt_proj_b, L0_attn_A_log, L0_attn_D_param,
           L0_attn_out_proj_W, L0_ffn_w1, L0_ffn_w2, L0_ffn_w3):
    f32 = lambda a: np.ascontiguousarray(np.asarray(a), dtype=np.float32)
    x = f32(L0_attn_feat_patch)
    w1 = f32(L0_ffn_w1)
    w2 = f32(L0_ffn_w2)
    w3 = f32(L0_ffn_w3)

    def wtiles(w):
        t = w.T.reshape(NCC, 128, NH, 128).transpose(2, 1, 0, 3)
        return np.ascontiguousarray(t).reshape(NH, 128, NCC * 128)

    shared = {
        "w1L": wtiles(w1),
        "w3L": wtiles(w3),
        "w2T": np.ascontiguousarray(w2.T).reshape(NH, 128, C),
        "inprojT": np.ascontiguousarray(f32(L0_attn_in_proj_W).T),
        "outprojT": np.ascontiguousarray(f32(L0_attn_out_proj_W).T),
        "xprojT": np.ascontiguousarray(f32(L0_attn_x_proj_W).T),
        "dtT": np.ascontiguousarray(f32(L0_attn_dt_proj_W).T),
        "Alog": f32(L0_attn_A_log),
        "Dp": f32(L0_attn_D_param).reshape(D, 1),
        "convw": f32(L0_attn_conv_w).reshape(D, 3),
        "convb": f32(L0_attn_conv_b).reshape(D, 1),
        "dtb": f32(L0_attn_dt_proj_b).reshape(D, 1),
        "lnw": np.ascontiguousarray(f32(L0_attn_ln_w).reshape(NCC, 128).T),
        "lnb": np.ascontiguousarray(f32(L0_attn_ln_b).reshape(NCC, 128).T),
    }
    in_maps = []
    for i in range(8):
        m = dict(shared)
        m["x"] = np.ascontiguousarray(x[i])
        in_maps.append(m)

    nc = _get_prog()
    res = run_bass_kernel_spmd(nc, in_maps, core_ids=list(range(8)))
    return np.stack([np.asarray(r["out"], dtype=np.float32)
                     for r in res.results], axis=0)


# version 25
# speedup vs baseline: 1.0232x; 1.0072x over previous
import numpy as np
from contextlib import ExitStack

import concourse.bass as bass
import concourse.mybir as mybir
from concourse.bass_utils import run_bass_kernel_spmd

F32 = mybir.dt.float32
BF16 = mybir.dt.bfloat16
I32 = mybir.dt.int32
AF = mybir.ActivationFunctionType
ALU = mybir.AluOpType
AXX = mybir.AxisListType.X

L = 2048
C = 2048
H = 5632
D = 96
NST = 16
RK = 6
EPS = 1e-5
NCC = C // 128
NH = H // 128
NTT = L // 128
QT = L // 4
TPQ = 4
NE = 8
TPE = 2
ET = L // NE
HALF = L // 2
TPH = 8
WN = float(H * C)
M2S = [7, 7, 7, 7, 7, 7, 2]
M2MAX = 7


class Sem:
    def __init__(self, es, nc, name):
        self.h = es.enter_context(nc.semaphore(name))
        self.v = 0

    def inc(self, inst, n=1):
        inst.then_inc(self.h, n)
        self.v += n
        return self.v


def build_core_program(stop_after=None):
    nc = bass.Bass("TRN2", target_bir_lowering=False)
    dr = {}

    def din(name, shp, dt=F32, kind="ExternalInput"):
        dr[name] = nc.dram_tensor(name, list(shp), dt, kind=kind).ap()
        return dr[name]

    din("x", (L, C))
    din("w1L", (NH, 128, NCC * 128))
    din("w3L", (NH, 128, NCC * 128))
    din("w2T", (NH, 128, C))
    din("inprojT", (C, D))
    din("outprojT", (D, C))
    din("xprojT", (D, RK + 2 * NST))
    din("dtT", (RK, D))
    din("Alog", (D, NST))
    din("Dp", (D, 1))
    din("convw", (D, 3))
    din("convb", (D, 1))
    din("dtb", (D, 1))
    din("lnw", (128, NCC))
    din("lnb", (128, NCC))
    din("out", (L, C), kind="ExternalOutput")
    din("h_scr", (H, L), kind="Internal")
    din("w2q_scr", (NH, 128, C), BF16, kind="Internal")

    es = ExitStack()
    SBm = lambda n, shp, dt=F32: es.enter_context(nc.sbuf_tensor(n, list(shp), dt))
    MKSEM = lambda n: Sem(es, nc, n)

    ident = SBm("ident", (128, 128))
    ones1_128 = SBm("ones1_128", (1, 128))
    ones1_96 = SBm("ones1_96", (1, 96))
    onescol = SBm("onescol", (128, 1))
    A_sb = SBm("A_sb", (D, NST))
    Dp_sb = SBm("Dp_sb", (D, 1))
    convw_sb = SBm("convw_sb", (D, 3))
    convb_sb = SBm("convb_sb", (D, 1))
    dtb_sb = SBm("dtb_sb", (D, 1))
    lnw_sb = SBm("lnw_sb", (128, NCC))
    lnb_sb = SBm("lnb_sb", (128, NCC))
    outprojT_sb = SBm("outprojT_sb", (D, C))
    xprojT_sb = SBm("xprojT_sb", (D, RK + 2 * NST))
    dtT_sb = SBm("dtT_sb", (RK, D))
    m1c = SBm("m1c", (128, NTT))
    nm1c = SBm("nm1c", (128, NTT))
    r1c = SBm("r1c", (128, NTT))
    stat2 = SBm("stat2", (128, 3 * NTT))
    nm2c = SBm("nm2c", (128, NTT))
    xcol = SBm("xcol", (128, 4 * NTT))
    ssum = SBm("ssum", (128, NTT))
    sq12 = SBm("sq12", (128, 2 * NTT))
    sxinvB = SBm("sxinvB", (128, L))
    y_sb = SBm("y_sb", (D, L))
    mtmp = SBm("mtmp", (128, 4 * NH))
    thrall = SBm("thrall", (128, 9))
    srow = SBm("srow", (1, 16))
    mxc2 = SBm("mxc2", (128, NTT))
    hscol = SBm("hscol", (128, NTT))
    fscr1 = SBm("fscr1", (128, C))
    cns = SBm("cns", (128, 4))
    wstg = [SBm(f"wstg{i}", (128, NCC * 128)) for i in range(2)]
    lnwb = SBm("lnwb", (128, 2 * NCC))

    qint = fscr1[:].bitcast(I32)
    s_ld = MKSEM("s_ld")
    s_x = MKSEM("s_x")
    s_xb = MKSEM("s_xb")
    s_w = [MKSEM("s_w0"), MKSEM("s_w1")]
    s_hd = [MKSEM("s_hd0"), MKSEM("s_hd1")]
    s_wq = MKSEM("s_wq")
    s_w2 = [MKSEM("s_w20"), MKSEM("s_w21")]
    s_wm = [MKSEM("s_wm0"), MKSEM("s_wm1")]
    s_t = MKSEM("s_t")
    s_v = MKSEM("s_v")
    s_a = MKSEM("s_a")
    s_g = MKSEM("s_g")
    s_o = [MKSEM("s_o0"), MKSEM("s_o1")]
    s_fl = MKSEM("s_fl")

    es_qx = ExitStack()
    qx = es_qx.enter_context(nc.sbuf_tensor("qx", [128, NCC * L], BF16,
                                            side="right"))

    es_xin = ExitStack()
    xin = es_xin.enter_context(nc.sbuf_tensor("xin", [D, L], F32))
    wmst = [wstg[0], wstg[1]] + \
        [es_xin.enter_context(nc.sbuf_tensor(f"wmst{i}", [128, NCC * 128],
                                             F32)) for i in range(2)]
    s_wm4 = [MKSEM(f"s_wm4_{i}") for i in range(4)]

    es_ph = ExitStack()
    SBp = lambda n, shp, dt=F32: es_ph.enter_context(nc.sbuf_tensor(n, list(shp), dt))
    iob = SBp("iob", (128, 128))
    inprojT_sb = SBp("inprojT_sb", (128, NCC * D))
    wfold = SBp("wfold", (128, D))
    xh2 = [SBp(f"xh2_{i}", (128, TPE * C)) for i in range(2)]
    pscr = SBp("pscr", (128, C))
    xnTc = SBp("xnTc", (128, 4 * ET))
    rowbuf = SBp("rowbuf", (2 * TPE, 256))
    rows = SBp("rows", (1, 4 * ET))
    sxB = SBp("sxB", (128, ET))
    r2B = SBp("r2B", (D, ET))
    rm2B = SBp("rm2B", (D, ET))
    ipcorr = SBp("ipcorr", (D, ET))
    bvec = SBp("bvec", (D, 2))

    ps_es = ExitStack()
    PS1 = lambda n, shp: ps_es.enter_context(nc.psum_tensor(n, list(shp), F32))
    ps_tr = [PS1(f"ps_tr{i}", (128, ET)) for i in range(4)]
    ps_bc = [PS1(f"ps_bc{i}", (128, ET)) for i in range(2)]
    ps_ip2 = [PS1(f"ps_ip{i}", (D, ET)) for i in range(2)]
    ps_m = ps_bc[0]

    with nc.Block("setup") as blk:
        @blk.sync
        def _(sy):
            for src, dst in (("Alog", A_sb), ("Dp", Dp_sb), ("convw", convw_sb),
                             ("convb", convb_sb), ("dtb", dtb_sb),
                             ("lnw", lnw_sb), ("lnb", lnb_sb),
                             ("outprojT", outprojT_sb), ("xprojT", xprojT_sb),
                             ("dtT", dtT_sb)):
                s_ld.inc(sy.dma_start(out=dst[:], in_=dr[src][:]), 16)
            s_ld.inc(sy.dma_start(
                out=inprojT_sb[:].rearrange("p (a d) -> p a d", a=NCC),
                in_=dr["inprojT"].rearrange("(a p) d -> p a d", p=128)), 16)

        @blk.gpsimd
        def _(g):
            g.iota(iob[:], [[1, 128]], channel_multiplier=-1,
                   allow_small_or_imprecise_dtypes=True)
            g.drain()
            g.tensor_single_scalar(ident[:], iob[:], 0.0, ALU.is_equal)
            g.drain()
            g.memset(ones1_128[:], 1.0)
            g.memset(ones1_96[:], 1.0)
            g.memset(onescol[:], 1.0)
            g.memset(y_sb[:], 0.0)
            g.memset(cns[:, 0:1], EPS)
            g.memset(cns[:, 1:2], 1.0 / C)
            g.memset(cns[:, 2:3], 0.001)
            g.memset(cns[:, 3:4], 1.0)
            g.drain()
            s_g.inc(g.tensor_copy(srow[0:1, 15:16], cns[0:1, 0:1]))
            g.drain()

        @blk.vector
        def _(v):
            v.wait_ge(s_ld.h, s_ld.v)
            v.tensor_copy(lnwb[:].rearrange("p (a two) -> p a two", two=2)[:, :, 0:1],
                          lnb_sb[:].unsqueeze(2))
            v.drain()
            s_v.inc(v.tensor_copy(
                lnwb[:].rearrange("p (a two) -> p a two", two=2)[:, :, 1:2],
                lnw_sb[:].unsqueeze(2)))
            v.drain()

        @blk.scalar
        def _(sc):
            sc.wait_ge(s_ld.h, s_ld.v)
            sc.activation(A_sb[:], A_sb[:], AF.Exp)
            sc.drain()
            s_a.inc(sc.activation(A_sb[:], A_sb[:], AF.Copy, scale=-1.0))
            sc.drain()

        lnwb_ok = s_v.v

        @blk.tensor
        def _(t):
            t.wait_ge(s_ld.h, s_ld.v)
            t.wait_ge(s_v.h, lnwb_ok)
            inst = None
            for cc in range(NCC):
                inst = t.matmul(ps_m[0:D, 0:2],
                                inprojT_sb[:, cc * D:(cc + 1) * D],
                                lnwb[:, 2 * cc:2 * cc + 2],
                                start=(cc == 0), stop=(cc == NCC - 1))
            s_t.inc(inst)
        bvc = s_t.v

        @blk.vector
        def _(v):
            v.wait_ge(s_t.h, bvc)
            v.tensor_copy(bvec[:, 0:2], ps_m[0:D, 0:2])
            v.drain()
            for cc in range(NCC):
                v.tensor_single_scalar(inprojT_sb[:, cc * D:(cc + 1) * D],
                                       inprojT_sb[:, cc * D:(cc + 1) * D],
                                       lnw_sb[:, cc:cc + 1], ALU.mult)
                v.drain()
            s_v.inc(v.tensor_copy(srow[0:1, 14:15], bvec[0:1, 0:1]))
            v.drain()
        fold_ok = s_v.v

    tr_cons = {0: None, 1: None, 2: None, 3: None}

    mean_units = [("w1L", i, i) for i in range(NH)] + \
                 [("w3L", i, NH + i) for i in range(NH)]
    mean_st = {"next": 0, "act": {}, "max": {}}

    def emit_means(blk, count, eng="scalar", defer=False, pre_incs=0):
        j0 = mean_st["next"]
        j1 = min(j0 + count, len(mean_units))
        if j1 == j0:
            return None
        mean_st["next"] = j1
        wld = {}
        sem = s_a if eng == "scalar" else s_v
        for j in range(j0, j1):
            mean_st["act"][j] = (sem, sem.v + pre_incs + (j - j0 + 1))
        mean_st["max"][id(sem)] = (sem, sem.v + pre_incs + (j1 - j0))

        @blk.gpsimd
        def _(g):
            for j in range(j0, j1):
                wname, i, col = mean_units[j]
                par = j % 4
                if j >= 4:
                    psem, pval = mean_st["act"][j - 4]
                    g.wait_ge(psem.h, pval)
                s_wm4[par].inc(g.dma_start(out=wmst[par][:],
                                           in_=dr[wname][i]), 16)
                wld[j] = s_wm4[par].v

        def body_sc(sc):
            for j in range(j0, j1):
                wname, i, col = mean_units[j]
                par = j % 4
                sc.wait_ge(s_wm4[par].h, wld[j])
                s_a.inc(sc.activation(wmst[par][:], wmst[par][:], AF.Abs,
                                      accum_out=mtmp[:, col:col + 1]))
                sc.drain()

        def body_ve(v):
            for j in range(j0, j1):
                wname, i, col = mean_units[j]
                par = j % 4
                v.wait_ge(s_wm4[par].h, wld[j])
                s_v.inc(v.tensor_reduce(mtmp[:, col:col + 1],
                                        wmst[par][:], AXX, ALU.add,
                                        apply_absolute_value=True))
                v.drain()

        body = body_sc if eng == "scalar" else body_ve
        if defer:
            return body
        if eng == "scalar":
            blk.scalar(body_sc)
        else:
            blk.vector(body_ve)
        return None

    def finalize_mean(blk, cols, thr_base, psum_ap, extra_wait=None):
        c0, c1 = cols

        @blk.vector
        def _(v):
            if extra_wait is not None:
                v.wait_ge(s_a.h, extra_wait)
            else:
                for sem, val in mean_st["max"].values():
                    v.wait_ge(sem.h, val)
            s_v.inc(v.tensor_reduce(mtmp[:, c0:c0 + 1], mtmp[:, c0:c1], AXX,
                                    ALU.add))
            v.drain()
        red = s_v.v

        @blk.tensor
        def _(t):
            t.wait_ge(s_v.h, red)
            s_t.inc(t.matmul(psum_ap[0:1, 0:1], mtmp[:, c0:c0 + 1], onescol[:],
                             start=True, stop=True))
        tot = s_t.v

        @blk.vector
        def _(v):
            v.wait_ge(s_t.h, tot)
            v.tensor_single_scalar(srow[0:1, 0:1], psum_ap[0:1, 0:1],
                                   1.0 / WN, ALU.mult)
            v.drain()
            v.tensor_single_scalar(srow[0:1, 1:2], srow[0:1, 0:1], EPS,
                                   ALU.max)
            v.drain()
            v.tensor_single_scalar(srow[0:1, 2:3], srow[0:1, 1:2], 0.5,
                                   ALU.mult)
            v.drain()
            v.tensor_single_scalar(srow[0:1, 3:4], srow[0:1, 1:2], -0.5,
                                   ALU.mult)
            v.drain()
            v.reciprocal(srow[0:1, 4:5], srow[0:1, 1:2])
            v.drain()
            v.reciprocal(srow[0:1, 5:6], srow[0:1, 4:5])
            v.drain()
            s_v.inc(v.tensor_single_scalar(srow[0:1, 5:6], srow[0:1, 5:6],
                                           0.5, ALU.mult))
            v.drain()
        srk = s_v.v

        @blk.tensor
        def _(t):
            t.wait_ge(s_v.h, srk)
            s_t.inc(t.matmul(psum_ap[:, 0:6], ones1_128[:], srow[0:1, 0:6],
                             start=True, stop=True))
        bck = s_t.v

        @blk.vector
        def _(v):
            v.wait_ge(s_t.h, bck)
            v.tensor_copy(thrall[:, thr_base:thr_base + 1], psum_ap[:, 2:3])
            v.drain()
            v.tensor_copy(thrall[:, thr_base + 1:thr_base + 2],
                          psum_ap[:, 3:4])
            v.drain()
            s_v.inc(v.tensor_copy(thrall[:, thr_base + 2:thr_base + 3],
                                  psum_ap[:, 5:6]))
            v.drain()
        return s_v.v

    s_x2 = [s_x, s_xb]
    tp_last = {}
    p7_hist = {}
    evac_hist = {}
    ipd_hist = {}
    trs1_hist = {}
    trs2_hist = {}
    rdma1 = {}
    rdma2 = {}
    bc2c_hist = {}
    x_ok = {}
    tr_cons = {}

    with nc.Block("ph01") as blk:
        def emit_xdma(e):
            @blk.sync
            def _(sy):
                if e >= 2:
                    sy.wait_ge(s_t.h, tp_last[e - 2])
                    sy.wait_ge(s_a.h, p7_hist[e - 2])
                s_x2[e % 2].inc(sy.dma_start(
                    out=xh2[e % 2][:].rearrange("p (a c) -> p a c", a=TPE),
                    in_=dr["x"][e * ET:(e + 1) * ET, :]
                    .rearrange("(a p) c -> p a c", p=128)), 16)
                x_ok[e] = s_x2[e % 2].v

        p3_hist = {}
        p4v_hist = {}
        bc1c_hist = {}
        rb1_hist = {}
        g2s_hist = {}

        def emit_head(e):
            t0 = e * TPE
            xb = xh2[e % 2]
            sb1 = 3 * NTT + (e % 2) * 8

            @blk.scalar
            def _(sc):
                sc.wait_ge(s_x2[e % 2].h, x_ok[e])
                for tt in range(TPE):
                    sc.activation(pscr[:], xb[:, tt * C:(tt + 1) * C],
                                  AF.Copy, accum_out=ssum[:, tt:tt + 1])
                    sc.drain()
                sc.activation(m1c[:, t0:t0 + TPE], ssum[:, 0:TPE], AF.Copy,
                              scale=1.0 / C)
                sc.drain()
                sc.activation(nm1c[:, t0:t0 + TPE], ssum[:, 0:TPE],
                              AF.Copy, scale=-1.0 / C)
                sc.drain()
                for tt in range(TPE):
                    sc.activation(pscr[:], xb[:, tt * C:(tt + 1) * C],
                                  AF.Square,
                                  bias=nm1c[:, t0 + tt:t0 + tt + 1],
                                  accum_out=ssum[:, tt:tt + 1])
                    sc.drain()
                s_a.inc(sc.activation(sq12[:, t0:t0 + TPE], ssum[:, 0:TPE],
                                      AF.Sqrt, scale=cns[:, 1:2],
                                      bias=cns[:, 0:1]))
                sc.drain()
            p3_hist[e] = s_a.v

            @blk.vector
            def _(v):
                v.wait_ge(s_a.h, p3_hist[e])
                if e >= 2:
                    v.wait_ge(s_t.h, trs1_hist[e - 2])
                v.reciprocal(r1c[:, t0:t0 + TPE], sq12[:, t0:t0 + TPE])
                v.drain()
                for tt in range(TPE):
                    v.tensor_scalar(xb[:, tt * C:(tt + 1) * C],
                                    xb[:, tt * C:(tt + 1) * C],
                                    m1c[:, t0 + tt:t0 + tt + 1],
                                    r1c[:, t0 + tt:t0 + tt + 1],
                                    ALU.subtract, ALU.mult)
                    v.drain()
                for tt in range(TPE):
                    v.tensor_reduce(
                        stat2[:, 2 * NTT + t0 + tt:2 * NTT + t0 + tt + 1],
                        xb[:, tt * C:(tt + 1) * C], AXX, ALU.max,
                        apply_absolute_value=True)
                    v.drain()
                v.tensor_single_scalar(xcol[:, t0:t0 + TPE],
                                       stat2[:, 2 * NTT + t0:
                                             2 * NTT + t0 + TPE],
                                       EPS, ALU.max)
                v.drain()
                v.reciprocal(xcol[:, t0:t0 + TPE], xcol[:, t0:t0 + TPE])
                v.drain()
                v.tensor_single_scalar(xcol[:, t0:t0 + TPE],
                                       xcol[:, t0:t0 + TPE], 127.0, ALU.mult)
                v.drain()
                v.reciprocal(xcol[:, NTT + t0:NTT + t0 + TPE],
                             xcol[:, t0:t0 + TPE])
                v.drain()
                v.tensor_copy(xcol[:, sb1:sb1 + TPE], xcol[:, t0:t0 + TPE])
                v.drain()
                s_v.inc(v.tensor_copy(xcol[:, sb1 + TPE:sb1 + 2 * TPE],
                                      xcol[:, NTT + t0:NTT + t0 + TPE]))
                v.drain()
            p4v_hist[e] = s_v.v

            @blk.tensor
            def _(t):
                t.wait_ge(s_v.h, p4v_hist[e])
                if e >= 2:
                    t.wait_ge(s_v.h, bc2c_hist[e - 2])
                s_t.inc(t.transpose(ps_bc[0][0:2 * TPE, 0:128],
                                    xcol[:, sb1:sb1 + 2 * TPE], ident[:]))
            trs1_hist[e] = s_t.v

            @blk.vector
            def _(v):
                v.wait_ge(s_t.h, trs1_hist[e])
                if e >= 1:
                    v.wait_ge(s_ld.h, rdma1[e - 1])
                s_v.inc(v.tensor_copy(rowbuf[0:2 * TPE, 0:128],
                                      ps_bc[0][0:2 * TPE, 0:128]))
                v.drain()
            rb1_hist[e] = s_v.v

            @blk.sync
            def _(sy):
                sy.wait_ge(s_v.h, rb1_hist[e])
                for k in range(2):
                    s_ld.inc(sy.dma_start(
                        out=rows[0:1, k * ET:(k + 1) * ET],
                        in_=rowbuf[k * TPE:(k + 1) * TPE, 0:128]), 16)
                rdma1[e] = s_ld.v

            @blk.tensor
            def _(t):
                t.wait_ge(s_ld.h, rdma1[e])
                t.matmul(ps_bc[0][:, 0:ET], ones1_128[:], rows[0:1, 0:ET],
                         start=True, stop=True)
                s_t.inc(t.matmul(ps_bc[1][:, 0:ET], ones1_128[:],
                                 rows[0:1, ET:2 * ET], start=True, stop=True))
            bc1 = s_t.v

            @blk.vector
            def _(v):
                v.wait_ge(s_t.h, bc1)
                v.tensor_copy(sxB[:], ps_bc[0][:, 0:ET])
                v.drain()
                s_v.inc(v.tensor_copy(sxinvB[:, e * ET:(e + 1) * ET],
                                      ps_bc[1][:, 0:ET]))
                v.drain()
            bc1c_hist[e] = s_v.v

        def emit_chunks(e):
            t0 = e * TPE
            xb = xh2[e % 2]
            bc1c = bc1c_hist[e]

            @blk.scalar
            def _(sc):
                sc.wait_ge(s_v.h, p4v_hist[e])
                for tt in range(TPE):
                    sc.activation(pscr[:], xb[:, tt * C:(tt + 1) * C],
                                  AF.Copy, accum_out=ssum[:, tt:tt + 1])
                    sc.drain()
                sc.activation(stat2[:, t0:t0 + TPE], ssum[:, 0:TPE], AF.Copy,
                              scale=1.0 / C)
                sc.drain()
                sc.activation(nm2c[:, t0:t0 + TPE], ssum[:, 0:TPE], AF.Copy,
                              scale=-1.0 / C)
                sc.drain()
                for tt in range(TPE):
                    sc.activation(pscr[:], xb[:, tt * C:(tt + 1) * C],
                                  AF.Square,
                                  bias=nm2c[:, t0 + tt:t0 + tt + 1],
                                  accum_out=ssum[:, tt:tt + 1])
                    sc.drain()
                s_a.inc(sc.activation(sq12[:, NTT + t0:NTT + t0 + TPE],
                                      ssum[:, 0:TPE], AF.Sqrt,
                                      scale=cns[:, 1:2], bias=cns[:, 0:1]))
                sc.drain()
            p7_hist[e] = s_a.v

            ipmm = {}
            ps_ip = ps_ip2[e % 2]
            for cc in range(NCC):
                par = (2, 3, 0, 1)[cc % 4]
                xsl = xnTc[:, cc % 4 * ET:(cc % 4 + 1) * ET]

                @blk.tensor
                def _(t):
                    if tr_cons.get(par) is not None:
                        t.wait_ge(s_v.h, tr_cons[par])
                    if cc == 0:
                        t.wait_ge(s_v.h, p4v_hist[e])
                        if e >= 2:
                            t.wait_ge(s_v.h, evac_hist[e - 2])
                    inst = None
                    for tt in range(TPE):
                        inst = t.transpose(
                            ps_tr[par][:, tt * 128:(tt + 1) * 128],
                            xb[:, tt * C + cc * 128:tt * C + (cc + 1) * 128],
                            ident[:])
                    s_t.inc(inst)
                tp = s_t.v

                @blk.vector
                def _(v):
                    v.wait_ge(s_t.h, tp)
                    if cc >= 4:
                        v.wait_ge(s_t.h, ipmm[cc - 4])
                    s_v.inc(v.tensor_copy(xsl, ps_tr[par][:]))
                    v.drain()
                ev1 = s_v.v
                tr_cons[par] = ev1

                @blk.vector
                def _(v):
                    if cc == 0:
                        v.wait_ge(s_v.h, bc1c)
                    v.tensor_tensor(qint[:, 0:ET], xsl, sxB[:], ALU.mult)
                    v.drain()
                    s_v.inc(v.tensor_copy(
                        qx[:, cc * L + e * ET:cc * L + (e + 1) * ET],
                        qint[:, 0:ET]))
                    v.drain()

                @blk.tensor
                def _(t):
                    t.wait_ge(s_v.h, ev1)
                    inst = t.matmul(ps_ip[:],
                                    inprojT_sb[:, cc * D:(cc + 1) * D],
                                    xsl,
                                    start=(cc == 0), stop=(cc == NCC - 1))
                    s_t.inc(inst)
                ipmm[cc] = s_t.v
            tp_last[e] = ipmm[NCC - 1]
            ipd_hist[e] = ipmm[NCC - 1]

        def emit_tail(e):
            t0 = e * TPE
            sb2 = 3 * NTT + (e % 2) * 8 + 4
            ps_ip = ps_ip2[e % 2]
            psb_free = bc1c_hist[min(e + 1, NE - 1)]

            @blk.vector
            def _(v):
                v.wait_ge(s_a.h, p7_hist[e])
                if e >= 2:
                    v.wait_ge(s_t.h, trs2_hist[e - 2])
                v.reciprocal(stat2[:, NTT + t0:NTT + t0 + TPE],
                             sq12[:, NTT + t0:NTT + t0 + TPE])
                v.drain()
                v.tensor_tensor(xcol[:, 2 * NTT + t0:2 * NTT + t0 + TPE],
                                stat2[:, t0:t0 + TPE],
                                stat2[:, NTT + t0:NTT + t0 + TPE], ALU.mult)
                v.drain()
                v.tensor_copy(xcol[:, sb2:sb2 + TPE],
                              stat2[:, NTT + t0:NTT + t0 + TPE])
                v.drain()
                s_v.inc(v.tensor_copy(xcol[:, sb2 + TPE:sb2 + 2 * TPE],
                                      xcol[:, 2 * NTT + t0:
                                            2 * NTT + t0 + TPE]))
                v.drain()
            g2s_hist[e] = s_v.v

            @blk.tensor
            def _(t):
                t.wait_ge(s_v.h, g2s_hist[e])
                t.wait_ge(s_v.h, psb_free)
                s_t.inc(t.transpose(ps_bc[0][0:2 * TPE, 0:128],
                                    xcol[:, sb2:sb2 + 2 * TPE], ident[:]))
            trs2_hist[e] = s_t.v

            @blk.vector
            def _(v):
                v.wait_ge(s_t.h, trs2_hist[e])
                if e >= 1:
                    v.wait_ge(s_ld.h, rdma2[e - 1])
                s_v.inc(v.tensor_copy(rowbuf[0:2 * TPE, 128:256],
                                      ps_bc[0][0:2 * TPE, 0:128]))
                v.drain()
            rb2 = s_v.v

            @blk.sync
            def _(sy):
                sy.wait_ge(s_v.h, rb2)
                for k in range(2):
                    s_ld.inc(sy.dma_start(
                        out=rows[0:1, (2 + k) * ET:(3 + k) * ET],
                        in_=rowbuf[k * TPE:(k + 1) * TPE, 128:256]), 16)
                rdma2[e] = s_ld.v

            @blk.tensor
            def _(t):
                t.wait_ge(s_ld.h, rdma2[e])
                t.matmul(ps_bc[0][0:D, 0:ET], ones1_96[:],
                         rows[0:1, 2 * ET:3 * ET], start=True, stop=True)
                s_t.inc(t.matmul(ps_bc[1][0:D, 0:ET], ones1_96[:],
                                 rows[0:1, 3 * ET:4 * ET], start=True,
                                 stop=True))
            bc2 = s_t.v

            @blk.vector
            def _(v):
                v.wait_ge(s_t.h, bc2)
                v.tensor_copy(r2B[:], ps_bc[0][0:D, 0:ET])
                v.drain()
                s_v.inc(v.tensor_copy(rm2B[:], ps_bc[1][0:D, 0:ET]))
                v.drain()
            bc2c_hist[e] = s_v.v

            @blk.vector
            def _(v):
                v.wait_ge(s_t.h, ipd_hist[e])
                v.tensor_tensor(fscr1[0:D, 0:ET], ps_ip[:], r2B[:], ALU.mult)
                v.drain()
                v.tensor_scalar(ipcorr[:], rm2B[:], bvec[:, 1:2],
                                bvec[:, 0:1], ALU.mult, ALU.subtract)
                v.drain()
                s_v.inc(v.tensor_tensor(xin[:, e * ET:(e + 1) * ET],
                                        fscr1[0:D, 0:ET], ipcorr[:],
                                        ALU.subtract))
                v.drain()
            evac_hist[e] = s_v.v

        emit_xdma(0)
        emit_xdma(1)
        emit_head(0)
        for e in range(NE):
            if e >= 1:
                emit_tail(e - 1)
            emit_chunks(e)
            if e + 2 < NE:
                emit_xdma(e + 2)
            if e + 1 < NE:
                emit_head(e + 1)
            if e >= 1:
                emit_means(blk, 3, "scalar")
        emit_tail(NE - 1)

    ps_es.close()
    es_ph.close()
    xin_ok = evac_hist[NE - 1]
    if stop_after == "ph1":
        es_xin.close(); es_qx.close(); es.close()
        return nc

    es_sc = ExitStack()
    SBs = lambda n, shp, dt=F32: es_sc.enter_context(nc.sbuf_tensor(n, list(shp), dt))
    sel = SBs("sel", (NST, NST * D))
    ucv = SBs("ucv", (D, L))
    bp_sb = SBs("bp_sb", (NST, L))
    cp_sb = SBs("cp_sb", (NST, L))
    delta = SBs("delta", (D, L))
    du = SBs("du", (D, L))
    dAb = [SBs(f"dAb{i}", (D, L)) for i in range(2)] + [fscr1]
    dbc_dr = delta
    dA_n = dAb[0]
    dBu_n = ucv
    st_n = dBu_n
    yprod = xin

    ps_es2 = ExitStack()
    PS2 = lambda n, shp: ps_es2.enter_context(nc.psum_tensor(n, list(shp), F32))
    ps_sc = PS2("ps_sc", (D, L))
    ps_cc = PS2("ps_cc", (D, L))

    with nc.Block("attn") as blk:
        @blk.gpsimd
        def _(g):
            g.iota(sel[:], [[1, NST], [0, D]], channel_multiplier=-1,
                   allow_small_or_imprecise_dtypes=True)
            g.drain()
            s_g.inc(g.tensor_single_scalar(sel[:], sel[:], 0.0, ALU.is_equal))
            g.drain()
        sel_ok = s_g.v

        @blk.vector
        def _(v):
            v.wait_ge(s_v.h, xin_ok)
            s_v.inc(v.tensor_scalar(ucv[:], xin[:], convw_sb[:, 1:2],
                                    convb_sb[:], ALU.mult, ALU.add))
            v.drain()
        c1d = s_v.v

        @blk.vector
        def _(v):
            v.wait_ge(s_v.h, c1d)
            v.scalar_tensor_tensor(ucv[:, 1:L], xin[:, 0:L - 1],
                                   convw_sb[:, 0:1], ucv[:, 1:L],
                                   ALU.mult, ALU.add)
            v.drain()
            s_v.inc(v.scalar_tensor_tensor(ucv[:, 0:L - 1], xin[:, 1:L],
                                           convw_sb[:, 2:3], ucv[:, 0:L - 1],
                                           ALU.mult, ALU.add))
            v.drain()
        cvd = s_v.v

        @blk.scalar
        def _(sc):
            sc.wait_ge(s_v.h, cvd)
            s_a.inc(sc.activation(ucv[:], ucv[:], AF.Silu))
            sc.drain()
        ud = s_a.v
        emit_means(blk, 4, "scalar")

        @blk.tensor
        def _(t):
            t.wait_ge(s_a.h, ud)
            inst = None
            for si, (c0, w, off) in enumerate(((0, RK, 0), (RK, NST, 32),
                                               (RK + NST, NST, 64))):
                for j in range(4):
                    inst = t.matmul(ps_sc[off:off + w, j * 512:(j + 1) * 512],
                                    xprojT_sb[:, c0:c0 + w],
                                    ucv[:, j * 512:(j + 1) * 512],
                                    start=True, stop=True)
            s_t.inc(inst)
        mmd = s_t.v

        @blk.vector
        def _(v):
            v.wait_ge(s_t.h, mmd)
            v.tensor_copy(dbc_dr[0:RK, :], ps_sc[0:RK, :])
            v.drain()
            v.tensor_copy(bp_sb[0:NST, :], ps_sc[32:32 + NST, :])
            v.drain()
            s_v.inc(v.tensor_copy(cp_sb[0:NST, :], ps_sc[64:64 + NST, :]))
            v.drain()
        xp_done = s_v.v

        @blk.tensor
        def _(t):
            t.wait_ge(s_v.h, xp_done)
            inst = None
            for j in range(4):
                inst = t.matmul(ps_sc[0:D, j * 512:(j + 1) * 512], dtT_sb[:],
                                dbc_dr[0:RK, j * 512:(j + 1) * 512],
                                start=True, stop=True)
            s_t.inc(inst)
        dtd = s_t.v

        sps = dAb[1]

        @blk.vector
        def _(v):
            v.wait_ge(s_t.h, dtd)
            s_v.inc(v.tensor_single_scalar(sps[:], ps_sc[0:D, :], dtb_sb[:],
                                           ALU.add))
            v.drain()
        zd = s_v.v

        @blk.scalar
        def _(sc):
            sc.wait_ge(s_v.h, zd)
            sc.activation(yprod[:], sps[:], AF.Relu)
            sc.drain()
            sc.activation(dA_n[:], sps[:], AF.Abs)
            sc.drain()
            sc.activation(dA_n[:], dA_n[:], AF.Exp, scale=-1.0)
            sc.drain()
            s_a.inc(sc.activation(dA_n[:], dA_n[:], AF.Ln, bias=cns[0:D, 3:4]))
            sc.drain()
        spd = s_a.v

        @blk.vector
        def _(v):
            v.wait_ge(s_a.h, spd)
            s_v.inc(v.tensor_tensor(sps[:], yprod[:], dA_n[:], ALU.add))
            v.drain()
        sp2d = s_v.v

        @blk.scalar
        def _(sc):
            sc.wait_ge(s_v.h, sp2d)
            s_a.inc(sc.activation(delta[:], sps[:], AF.Copy,
                                  scale=cns[0:D, 2:3]))
            sc.drain()
        dltd = s_a.v

        @blk.vector
        def _(v):
            v.wait_ge(s_a.h, dltd)
            v.tensor_tensor(du[:], delta[:], ucv[:], ALU.mult)
            v.drain()
            s_v.inc(v.scalar_tensor_tensor(y_sb[:], ucv[:], Dp_sb[:], y_sb[:],
                                           ALU.mult, ALU.add))
            v.drain()
        dud = s_v.v

        dAd = {}

        def emit_exp(sc, n):
            s_a.inc(sc.activation(dAb[n % 3][0:D, :], delta[:], AF.Exp,
                                  scale=A_sb[:, n:n + 1]))
            sc.drain()
            dAd[n] = s_a.v

        @blk.scalar
        def _(sc):
            for n in range(3):
                emit_exp(sc, n)
        emit_means(blk, 8, "scalar")

        prev_dBu = None
        prev_yadd = None
        yadds = {}
        for n in range(NST):
            @blk.tensor
            def _(t):
                if n == 0:
                    t.wait_ge(s_v.h, dud)
                    t.wait_ge(s_g.h, sel_ok)
                else:
                    t.wait_ge(s_v.h, prev_dBu)
                inst = None
                for j in range(4):
                    inst = t.matmul(ps_sc[0:D, j * 512:(j + 1) * 512],
                                    sel[:, n * D:(n + 1) * D],
                                    bp_sb[:, j * 512:(j + 1) * 512],
                                    start=True, stop=True)
                s_t.inc(inst)
            bBd = s_t.v

            @blk.vector
            def _(v):
                v.wait_ge(s_t.h, bBd)
                s_v.inc(v.tensor_tensor(dBu_n[:], du[:], ps_sc[0:D, :],
                                        ALU.mult))
                v.drain()
            prev_dBu = s_v.v

            @blk.tensor
            def _(t):
                if prev_yadd is not None:
                    t.wait_ge(s_v.h, prev_yadd)
                inst = None
                for j in range(4):
                    inst = t.matmul(ps_cc[0:D, j * 512:(j + 1) * 512],
                                    sel[:, n * D:(n + 1) * D],
                                    cp_sb[:, j * 512:(j + 1) * 512],
                                    start=True, stop=True)
                s_t.inc(inst)
            bCd = s_t.v

            @blk.vector
            def _(v):
                v.wait_ge(s_a.h, dAd[n])
                v.tensor_tensor_scan(st_n[:], dAb[n % 3][0:D, :], dBu_n[:],
                                     0.0, ALU.mult, ALU.add)
                v.drain()
                v.wait_ge(s_t.h, bCd)
                v.tensor_tensor(yprod[:], st_n[:], ps_cc[0:D, :], ALU.mult)
                v.drain()
                s_v.inc(v.tensor_tensor(y_sb[:], y_sb[:], yprod[:], ALU.add))
                v.drain()
            prev_yadd = s_v.v
            yadds[n] = s_v.v

            mbody = emit_means(blk, 4, "scalar", defer=True,
                               pre_incs=(1 if n + 3 < NST else 0))
            if n + 3 < NST or mbody is not None:
                @blk.scalar
                def _(sc):
                    sc.wait_ge(s_v.h, yadds[n])
                    if n + 3 < NST:
                        emit_exp(sc, n + 3)
                    if mbody is not None:
                        mbody(sc)

        emit_means(blk, len(mean_units) - mean_st["next"])

    ps_es2.close()
    es_sc.close()
    es_xin.close()
    if stop_after == "scan":
        es_qx.close(); es.close()
        return nc

    es_mm = ExitStack()
    SBmm = lambda n, shp, dt=F32: es_mm.enter_context(nc.sbuf_tensor(n, list(shp), dt))
    fscr2 = SBmm("fscr2", (128, C))
    hbuf = [SBmm(f"hbuf{i}", (128, HALF)) for i in range(2)]
    wstg2 = [SBmm(f"wstg2{i}", (128, NCC * 128)) for i in range(2)]
    shB = SBmm("shB", (128, L))

    if stop_after == "means":
        es_mm.close(); es_qx.close(); es.close()
        return nc

    es_m13 = ExitStack()
    w1q = [es_m13.enter_context(nc.sbuf_tensor(f"w1q{i}", [128, NCC * 128], BF16))
           for i in range(2)]
    w2m = [es_m13.enter_context(nc.sbuf_tensor(f"w2m{i}", [128, HALF], F32))
           for i in range(2)]
    gscr = es_m13.enter_context(nc.sbuf_tensor("gscr", [128, NCC * 128], BF16))
    gscr2 = es_m13.enter_context(nc.sbuf_tensor("gscr2", [128, NCC * 128], BF16))
    gscr3 = es_m13.enter_context(nc.sbuf_tensor("gscr3", [128, NCC * 128], BF16))
    gscr4 = es_m13.enter_context(nc.sbuf_tensor("gscr4", [128, NCC * 128], BF16))
    w3q = [es_m13.enter_context(nc.sbuf_tensor(f"w3q{i}", [128, NCC * 128], BF16))
           for i in range(2)]
    hmax = es_m13.enter_context(nc.sbuf_tensor("hmax", [128, L], F32))

    ps_es3 = ExitStack()
    PS3 = lambda n, shp: ps_es3.enter_context(nc.psum_tensor(n, list(shp), F32))
    ps_a = [PS3(f"ps_a{i}", (128, HALF)) for i in range(2)]
    ps_b = [PS3(f"ps_b{i}", (128, HALF)) for i in range(2)]

    with nc.Block("m13") as blk:
        @blk.gpsimd
        def _(g):
            s_g.inc(g.memset(hmax[:], 0.0))
            g.drain()
        hmz = s_g.v

        thr13 = finalize_mean(blk, (0, NH), 0, ps_a[0])
        thr13 = finalize_mean(blk, (NH, 2 * NH), 3, ps_a[0])

        quant_done = {}
        quant_vec = {}
        evac_done = {}
        hbuf_free = {}
        mm_done_per_ht = {}
        w2m_act = {}
        w2m_sems = {}
        w3add_hist = {}
        w1add_hist = {}

        def emit_quant(ht):
            par = ht % 2

            @blk.sync
            def _(sy):
                if ht >= 2:
                    sy.wait_ge(s_v.h, quant_done[ht - 2])
                s_w[par].inc(sy.dma_start(out=wstg[par][:],
                                          in_=dr["w1L"][ht]), 16)
                s_w2[par].inc(sy.dma_start(out=wstg2[par][:],
                                           in_=dr["w3L"][ht]), 16)
            w1ld = s_w[par].v
            w3ld = s_w2[par].v

            @blk.scalar
            def _(sc):
                sc.wait_ge(s_w[par].h, w1ld)
                if ht < 2:
                    sc.wait_ge(s_v.h, thr13)
                if ht >= 1:
                    sc.wait_ge(s_v.h, w1add_hist[ht - 1])
                sc.activation(gscr[:], wstg[par][:], AF.Sign,
                              bias=thrall[:, 1:2])
                sc.drain()
                s_a.inc(sc.activation(gscr2[:], wstg[par][:], AF.Sign,
                                      bias=thrall[:, 0:1]))
                sc.drain()
                if ht >= 1:
                    sc.wait_ge(s_v.h, w3add_hist[ht - 1])
                sc.wait_ge(s_w2[par].h, w3ld)
                sc.activation(gscr3[:], wstg2[par][:], AF.Sign,
                              bias=thrall[:, 4:5])
                sc.drain()
                s_a.inc(sc.activation(gscr4[:], wstg2[par][:], AF.Sign,
                                      bias=thrall[:, 3:4]))
                sc.drain()
            w3sg = s_a.v
            w1sg = w3sg - 1

            @blk.vector
            def _(v):
                v.wait_ge(s_a.h, w1sg)
                if ht >= 2:
                    v.wait_ge(s_t.h, mm_done_per_ht[ht - 2])
                s_v.inc(v.tensor_tensor(w1q[par][:], gscr[:], gscr2[:],
                                        ALU.add))
                v.drain()
            w1add = s_v.v
            w1add_hist[ht] = s_v.v

            @blk.vector
            def _(v):
                v.wait_ge(s_a.h, w3sg)
                s_v.inc(v.tensor_tensor(w3q[par][:], gscr3[:], gscr4[:],
                                        ALU.add))
                v.drain()
            quant_done[ht] = s_v.v
            quant_vec[ht] = s_v.v
            w3add_hist[ht] = s_v.v

        def emit_units(ht):
            par = ht % 2
            for th in range(2):
                u = 2 * ht + th
                pu = u % 2

                @blk.tensor
                def _(t):
                    t.wait_ge(s_v.h, quant_done[ht])
                    if u >= 2:
                        t.wait_ge(s_v.h, evac_done[u - 2])
                    for cc in range(NCC):
                        for j in range(2):
                            t.matmul(ps_a[pu][:, j * 512:(j + 1) * 512],
                                     w1q[par][:, cc * 128:(cc + 1) * 128],
                                     qx[:, cc * L + th * HALF + j * 512:
                                        cc * L + th * HALF + (j + 1) * 512],
                                     start=(cc == 0), stop=(cc == NCC - 1))
                    inst = None
                    for cc in range(NCC):
                        for j in range(2):
                            inst = t.matmul(
                                ps_b[pu][:, j * 512:(j + 1) * 512],
                                w3q[par][:, cc * 128:(cc + 1) * 128],
                                qx[:, cc * L + th * HALF + j * 512:
                                   cc * L + th * HALF + (j + 1) * 512],
                                start=(cc == 0), stop=(cc == NCC - 1))
                    s_t.inc(inst)
                mmd = s_t.v
                mm_done_per_ht[ht] = mmd

                @blk.vector
                def _(v):
                    v.wait_ge(s_t.h, mmd)
                    if pu in hbuf_free:
                        v.wait_ge(s_hd[pu].h, hbuf_free[pu][0])
                        v.wait_ge(s_v.h, hbuf_free[pu][1])
                    s_v.inc(v.tensor_tensor(
                        fscr1[:, 0:HALF], ps_a[pu][:],
                        sxinvB[:, th * HALF:(th + 1) * HALF], ALU.mult))
                    v.drain()
                ta_ok = s_v.v

                @blk.scalar
                def _(sc):
                    sc.wait_ge(s_v.h, ta_ok)
                    s_a.inc(sc.activation(fscr2[:, 0:HALF], fscr1[:, 0:HALF],
                                          AF.Silu, scale=thrall[:, 2:3]))
                    sc.drain()
                sld = s_a.v

                @blk.vector
                def _(v):
                    v.wait_ge(s_a.h, sld)
                    v.tensor_tensor(fscr1[:, 0:HALF], ps_b[pu][:],
                                    sxinvB[:, th * HALF:(th + 1) * HALF],
                                    ALU.mult)
                    v.drain()
                    s_v.inc(v.scalar_tensor_tensor(
                        hbuf[pu][:], fscr1[:, 0:HALF], thrall[:, 5:6],
                        fscr2[:, 0:HALF], ALU.mult, ALU.mult))
                    v.drain()
                evac_done[u] = s_v.v
                h_ok = s_v.v

                @blk.gpsimd
                def _(g):
                    g.wait_ge(s_v.h, h_ok)
                    s_hd[pu].inc(g.dma_start(
                        out=dr["h_scr"][ht * 128:(ht + 1) * 128,
                                        th * HALF:(th + 1) * HALF],
                        in_=hbuf[pu][:]), 16)

                @blk.vector
                def _(v):
                    if u == 0:
                        v.wait_ge(s_g.h, hmz)
                    v.scalar_tensor_tensor(fscr1[:, 0:HALF], hbuf[pu][:],
                                           -1.0, hbuf[pu][:],
                                           ALU.mult, ALU.max)
                    v.drain()
                    s_v.inc(v.tensor_tensor(
                        hmax[:, th * HALF:(th + 1) * HALF],
                        hmax[:, th * HALF:(th + 1) * HALF],
                        fscr1[:, 0:HALF], ALU.max))
                    v.drain()
                hbuf_free[pu] = (s_hd[pu].v, s_v.v)

        def emit_w2mean(ht):
            for hf in range(2):
                j = 2 * ht + hf
                wpar = j % 2

                @blk.sync
                def _(sy):
                    if j >= 2:
                        sy.wait_ge(s_a.h, w2m_act[j - 2])
                    s_wm[wpar].inc(sy.dma_start(
                        out=w2m[wpar][:],
                        in_=dr["w2T"][ht][:, hf * HALF:(hf + 1) * HALF]), 16)
                w2m_sems[j] = s_wm[wpar].v

                @blk.scalar
                def _(sc):
                    sc.wait_ge(s_wm[wpar].h, w2m_sems[j])
                    s_a.inc(sc.activation(
                        w2m[wpar][:], w2m[wpar][:], AF.Abs,
                        accum_out=mtmp[:, (2 + hf) * NH + ht:
                                       (2 + hf) * NH + ht + 1]))
                    sc.drain()
                w2m_act[j] = s_a.v

        for ht in range(NH + 1):
            if ht < NH:
                emit_quant(ht)
            if ht >= 1:
                emit_units(ht - 1)
            if ht < NH:
                emit_w2mean(ht)

    hstore_total = [s_hd[0].v, s_hd[1].v]
    es_qx.close()
    if stop_after == "m13":
        ps_es3.close(); es_m13.close(); es_mm.close(); es.close()
        return nc

    ps_es3.close()

    es_hf = ExitStack()
    rowb2 = es_hf.enter_context(nc.sbuf_tensor("rowb2", [NTT, 128], F32))
    rows2 = es_hf.enter_context(nc.sbuf_tensor("rows2", [1, 2 * L], F32))
    ps_es4 = ExitStack()
    PS4 = lambda n, shp: ps_es4.enter_context(nc.psum_tensor(n, list(shp), F32))
    ps_op = PS4("ps_op", (128, C))
    ps_f = [PS4(f"ps_f{i}", (128, HALF)) for i in range(2)]

    with nc.Block("hfin") as blk:
        thrw2 = finalize_mean(blk, (2 * NH, 4 * NH), 6, ps_op,
                              extra_wait=w2m_act[2 * NH - 1])

        @blk.tensor
        def _(t):
            t.wait_ge(s_v.h, thrw2)
            inst = None
            for j in range(NTT):
                inst = t.transpose(ps_op[:, j * 128:(j + 1) * 128],
                                   hmax[:, j * 128:(j + 1) * 128], ident[:])
            s_t.inc(inst)
        hts = s_t.v

        @blk.vector
        def _(v):
            v.wait_ge(s_t.h, hts)
            for j in range(NTT):
                v.tensor_reduce(mxc2[:, j:j + 1],
                                ps_op[:, j * 128:(j + 1) * 128], AXX, ALU.max)
                v.drain()
            v.tensor_single_scalar(hscol[:], mxc2[:], EPS, ALU.max)
            v.drain()
            v.reciprocal(hscol[:], hscol[:])
            v.drain()
            v.tensor_single_scalar(hscol[:], hscol[:], 127.0, ALU.mult)
            v.drain()
            v.reciprocal(hscol[:], hscol[:])
            v.drain()
            s_v.inc(v.tensor_single_scalar(hscol[:], hscol[:],
                                           thrall[:, 8:9], ALU.mult))
            v.drain()
        hcol_ok = s_v.v

        @blk.tensor
        def _(t):
            t.wait_ge(s_v.h, hcol_ok)
            s_t.inc(t.transpose(ps_op[0:NTT, 0:128], mxc2[:], ident[:]))
        ht2 = s_t.v

        @blk.vector
        def _(v):
            v.wait_ge(s_t.h, ht2)
            s_v.inc(v.tensor_copy(rowb2[:], ps_op[0:NTT, 0:128]))
            v.drain()
        rb2 = s_v.v

        @blk.gpsimd
        def _(g):
            g.wait_ge(s_v.h, rb2)
            s_fl.inc(g.dma_start(out=rows2[0:1, 0:L], in_=rowb2[:]), 16)
        r2ok = s_fl.v

        @blk.vector
        def _(v):
            v.wait_ge(s_fl.h, r2ok)
            v.tensor_single_scalar(rows2[0:1, L:2 * L], rows2[0:1, 0:L], EPS,
                                   ALU.max)
            v.drain()
            v.reciprocal(rows2[0:1, L:2 * L], rows2[0:1, L:2 * L])
            v.drain()
            s_v.inc(v.tensor_single_scalar(rows2[0:1, L:2 * L],
                                           rows2[0:1, L:2 * L], 127.0,
                                           ALU.mult))
            v.drain()
        shrow_ok = s_v.v

        @blk.tensor
        def _(t):
            t.wait_ge(s_v.h, shrow_ok)
            inst = None
            for j in range(4):
                inst = t.matmul(ps_op[:, j * 512:(j + 1) * 512], ones1_128[:],
                                rows2[0:1, L + j * 512:L + (j + 1) * 512],
                                start=True, stop=True)
            s_t.inc(inst)
        shb = s_t.v

        @blk.vector
        def _(v):
            v.wait_ge(s_t.h, shb)
            s_v.inc(v.tensor_copy(shB[:], ps_op[:]))
            v.drain()
    shB_ok = s_v.v
    es_hf.close()
    es_m13.close()

    M2SB = 4
    NSB2 = 11
    es_m2 = ExitStack()
    f_half = es_m2.enter_context(nc.sbuf_tensor("f_half", [128, TPH * C], F32))
    w2s = [es_m2.enter_context(nc.sbuf_tensor(f"w2s{i}", [128, M2SB * C], BF16))
           for i in range(2)]
    qhs = [es_m2.enter_context(nc.sbuf_tensor(f"qhs{i}", [128, M2SB * HALF],
                                              BF16)) for i in range(2)]
    w4stg = [wstg[0], wstg[1], sxinvB]
    s_wx = [MKSEM("s_wx0")]
    s_w4s = [s_w[0], s_w[1], s_wx[0]]

    xre2 = [wstg2[0], wstg[0]]
    gbuf2 = [wstg2[1], wstg[1]]

    wq_sems = {}
    hq_last = {0: None, 1: None}
    prev_store = [0, 0]
    for half in range(2):
        hq_sems = {}
        hm1_sems = {}
        with nc.Block(f"m2_{half}") as blk:
            mm_last = {}
            slab_ok = {}
            prev_sb_cons = {}

            prep_late = {}

            def emit_prep_early(sbi):
                sp = sbi % 2
                sbn = min(M2SB, NH - sbi * M2SB)
                base = sbi * M2SB
                sgb = lambda ht: (fscr2[:].bitcast(BF16) if ht % 2 == 0
                                  else wstg2[0][:].bitcast(BF16))
                slab_v = 0
                slab_w = [0, 0]

                nsg = sbn if half == 0 else 0
                a0 = s_a.v
                v0h = s_v.v
                for k in range(sbn):
                    if k < 2:
                        hq_sems[base + k] = a0 + k + 1
                    else:
                        hq_sems[base + k] = a0 + 2 + nsg + (k - 1)
                    hm1_sems[base + k] = v0h + k + 1
                if half == 0:
                    for k in range(min(2, sbn)):
                        wq_sems[base + k] = v0h + sbn + k + 1

                hlds = {}

                @blk.gpsimd
                def _(g):
                    for k in range(sbn):
                        ht = base + k
                        par = ht % 2
                        if ht >= 2:
                            g.wait_ge(s_v.h, hm1_sems[ht - 2])
                        elif half == 0:
                            g.wait_ge(s_hd[par].h, hstore_total[par])
                        else:
                            g.wait_ge(s_a.h, hq_last[par])
                        s_hd[par].inc(g.dma_start(
                            out=hbuf[par][:],
                            in_=dr["h_scr"][ht * 128:(ht + 1) * 128,
                                            half * HALF:(half + 1) * HALF]),
                            16)
                        hlds[k] = s_hd[par].v

                wlds = {}
                if half == 0:
                    @blk.sync
                    def _(sy):
                        for k in range(sbn):
                            ht = base + k
                            p4 = ht % 3
                            if ht >= 3:
                                sy.wait_ge(s_v.h, wq_sems[ht - 3])
                            s_w4s[p4].inc(sy.dma_start(
                                out=w4stg[p4][:], in_=dr["w2T"][ht]), 16)
                            wlds[k] = s_w4s[p4].v
                else:
                    @blk.sync
                    def _(sy):
                        for k in range(sbn):
                            ht = base + k
                            if sbi == 0 and k == 0:
                                sy.wait_ge(s_wq.h, s_wq.v)
                            if k == 0 and sbi >= 2:
                                sy.wait_ge(s_t.h, mm_last[sbi - 2])
                            s_w[sp].inc(sy.dma_start(
                                out=w2s[sp][:, k * C:(k + 1) * C],
                                in_=dr["w2q_scr"][ht]), 16)
                    slab_w[sp] = s_w[sp].v
                    prep_late[sbi] = None

                hm1s = {}

                @blk.vector
                def _(v):
                    for k in range(sbn):
                        ht = base + k
                        par = ht % 2
                        qv = (qint[:, 0:HALF] if ht % 2 == 0
                              else qint[:, HALF:2 * HALF])
                        v.wait_ge(s_hd[par].h, hlds[k])
                        if ht >= 2:
                            v.wait_ge(s_a.h, hq_sems[ht - 2])
                        if half == 0 and ht == 0:
                            v.wait_ge(s_v.h, shB_ok)
                        got = s_v.inc(v.tensor_tensor(
                            qv, hbuf[par][:],
                            shB[:, half * HALF:(half + 1) * HALF],
                            ALU.mult))
                        v.drain()
                        assert got == hm1_sems[ht]
                        hm1s[k] = got

                def emit_copies(kr):
                    @blk.scalar
                    def _(sc):
                        for k in kr:
                            ht = base + k
                            qv = (qint[:, 0:HALF] if ht % 2 == 0
                                  else qint[:, HALF:2 * HALF])
                            sc.wait_ge(s_v.h, hm1s[k])
                            if k == 0 and sbi >= 2:
                                sc.wait_ge(s_t.h, mm_last[sbi - 2])
                            got = s_a.inc(sc.activation(
                                qhs[sp][:, k * HALF:(k + 1) * HALF], qv,
                                AF.Copy))
                            sc.drain()
                            assert got == hq_sems[ht]

                emit_copies(range(0, min(2, sbn)))

                if half == 0:
                    sgs = {}

                    @blk.scalar
                    def _(sc):
                        for k in range(sbn):
                            ht = base + k
                            p4 = ht % 3
                            sc.wait_ge(s_w4s[p4].h, wlds[k])
                            if ht >= 2 and (ht - 2) in wq_sems:
                                sc.wait_ge(s_v.h, wq_sems[ht - 2])
                            sc.activation(sgb(ht)[:, 0:C], w4stg[p4][:],
                                          AF.Sign, bias=thrall[:, 7:8])
                            sc.drain()
                            got = s_a.inc(sc.activation(
                                sgb(ht)[:, C:2 * C], w4stg[p4][:], AF.Sign,
                                bias=thrall[:, 6:7]))
                            sc.drain()
                            sgs[k] = got

                if sbn > 2:
                    emit_copies(range(2, sbn))
                slab_a = hq_sems[base + sbn - 1]

                if half == 0:
                    for k in range(min(2, sbn)):
                        assert wq_sems[base + k] == s_v.v + k + 1

                    def emit_adds(kr):
                        @blk.vector
                        def _(v):
                            for k in kr:
                                ht = base + k
                                v.wait_ge(s_a.h, sgs[k])
                                if k == 0 and sbi >= 2:
                                    v.wait_ge(s_t.h, mm_last[sbi - 2])
                                    v.wait_ge(s_wq.h, s_wq.v)
                                got = s_v.inc(v.tensor_tensor(
                                    w2s[sp][:, k * C:(k + 1) * C],
                                    sgb(ht)[:, 0:C], sgb(ht)[:, C:2 * C],
                                    ALU.add))
                                v.drain()
                                if ht in wq_sems:
                                    assert got == wq_sems[ht]
                                else:
                                    wq_sems[ht] = got

                    emit_adds(range(0, min(2, sbn)))

                    def late():
                        if sbn > 2:
                            emit_adds(range(2, sbn))
                        slab_ok[sbi][0] = wq_sems[base + sbn - 1]

                        @blk.sync
                        def _(sy):
                            for k in range(sbn):
                                ht = base + k
                                sy.wait_ge(s_v.h, wq_sems[ht])
                                s_wq.inc(sy.dma_start(
                                    out=dr["w2q_scr"][ht],
                                    in_=w2s[sp][:, k * C:(k + 1) * C]), 16)
                    prep_late[sbi] = late

                slab_ok[sbi] = [slab_v, slab_a, list(slab_w)]

            def emit_m2_units(sbi):
                sp = sbi % 2
                sbn = min(M2SB, NH - sbi * M2SB)
                slab_v, slab_a, slab_w = slab_ok[sbi]
                ps_cons = {}
                for tt in range(TPH):
                    for ch in range(2):
                        un = tt * 2 + ch
                        pu = un % 2

                        @blk.tensor
                        def _(t):
                            if un == 0:
                                if slab_v:
                                    t.wait_ge(s_v.h, slab_v)
                                if slab_a:
                                    t.wait_ge(s_a.h, slab_a)
                                for pp in range(2):
                                    if slab_w[pp]:
                                        t.wait_ge(s_w[pp].h, slab_w[pp])
                            if un >= 2:
                                t.wait_ge(s_v.h, ps_cons[un - 2])
                            elif sbi > 0:
                                t.wait_ge(s_v.h,
                                          prev_sb_cons[2 * TPH - 2 + un])
                            inst = None
                            for k in range(sbn):
                                for j in range(2):
                                    inst = t.matmul(
                                        ps_f[pu][:, j * 512:(j + 1) * 512],
                                        qhs[sp][:, k * HALF + tt * 128:
                                                k * HALF + (tt + 1) * 128],
                                        w2s[sp][:, k * C + ch * 1024 + j * 512:
                                                k * C + ch * 1024 +
                                                (j + 1) * 512],
                                        start=(k == 0), stop=(k == sbn - 1))
                            s_t.inc(inst)
                        mmk = s_t.v

                        @blk.vector
                        def _(v):
                            v.wait_ge(s_t.h, mmk)
                            dst = f_half[:, tt * C + ch * 1024:
                                         tt * C + (ch + 1) * 1024]
                            if sbi == 0:
                                s_v.inc(v.tensor_copy(dst, ps_f[pu][:]))
                            else:
                                s_v.inc(v.tensor_tensor(dst, dst, ps_f[pu][:],
                                                        ALU.add))
                            v.drain()
                        ps_cons[un] = s_v.v
                mm_last[sbi] = s_t.v
                prev_sb_cons.clear()
                prev_sb_cons.update(ps_cons)

            for sbi in range(NSB2 + 1):
                if sbi < NSB2:
                    emit_prep_early(sbi)
                if sbi >= 1:
                    emit_m2_units(sbi - 1)
                if sbi < NSB2 and prep_late[sbi] is not None:
                    prep_late[sbi]()
            hq_last[0] = hq_sems[NH - 2]
            hq_last[1] = hq_sems[NH - 1]
            f_ok = s_v.v
            fwait = dict(prev_sb_cons)

            ob_hist = {}
            st_hist = {}
            for tt in range(TPH):
                gt = half * TPH + tt
                fp = tt % 2
                xre = xre2[fp]
                gbuf = gbuf2[fp]

                @blk.sync
                def _(sy):
                    sy.wait_ge(s_v.h, ob_hist.get(tt - 2, fwait[2 * tt + 1]))
                    s_x.inc(sy.dma_start(
                        out=xre[:], in_=dr["x"][gt * 128:(gt + 1) * 128, :]),
                        16)
                xld = s_x.v

                @blk.tensor
                def _(t):
                    t.wait_ge(s_v.h, fwait[2 * tt + 1])
                    if tt >= 1:
                        t.wait_ge(s_v.h, ob_hist[tt - 1])
                    inst = None
                    for j in range(4):
                        inst = t.matmul(ps_op[:, j * 512:(j + 1) * 512],
                                        y_sb[:, gt * 128:(gt + 1) * 128],
                                        outprojT_sb[:, j * 512:(j + 1) * 512],
                                        start=True, stop=True)
                    s_t.inc(inst)
                opd = s_t.v

                @blk.scalar
                def _(sc):
                    sc.wait_ge(s_v.h, fwait[2 * tt + 1])
                    sc.wait_ge(s_o[fp].h, st_hist.get(tt - 2, prev_store[fp]))
                    s_a.inc(sc.activation(gbuf[:],
                                          f_half[:, tt * C:(tt + 1) * C],
                                          AF.Silu, scale=hscol[:, gt:gt + 1]))
                    sc.drain()
                gok = s_a.v

                @blk.vector
                def _(v):
                    v.wait_ge(s_x.h, xld)
                    v.wait_ge(s_t.h, opd)
                    v.wait_ge(s_a.h, gok)
                    v.tensor_scalar(fscr1[:], xre[:], m1c[:, gt:gt + 1],
                                    r1c[:, gt:gt + 1], ALU.subtract, ALU.mult)
                    v.drain()
                    v.tensor_tensor(fscr1[:], ps_op[:], fscr1[:], ALU.add)
                    v.drain()
                    v.tensor_tensor(fscr2[:], fscr1[:], gbuf[:], ALU.mult)
                    v.drain()
                    s_v.inc(v.tensor_tensor(gbuf[:], fscr2[:], xre[:],
                                            ALU.add))
                    v.drain()
                ob_hist[tt] = s_v.v

                @blk.sync
                def _(sy):
                    sy.wait_ge(s_v.h, ob_hist[tt])
                    s_o[fp].inc(sy.dma_start(
                        out=dr["out"][gt * 128:(gt + 1) * 128, :],
                        in_=gbuf[:]), 16)
                st_hist[tt] = s_o[fp].v
            prev_store = [st_hist[TPH - 2], st_hist[TPH - 1]]

    with nc.Block("fini") as blk:
        @blk.sync
        def _(sy):
            sy.wait_ge(s_o[0].h, s_o[0].v)
            sy.wait_ge(s_o[1].h, s_o[1].v)

    ps_es4.close()
    es_m2.close()
    es_mm.close()
    es.close()
    return nc


_PROG_CACHE = {}


def _get_prog():
    if "nc" not in _PROG_CACHE:
        _PROG_CACHE["nc"] = build_core_program()
    return _PROG_CACHE["nc"]


def kernel(L0_attn_feat_patch, L0_attn_ln_w, L0_attn_ln_b, L0_attn_in_proj_W,
           L0_attn_conv_w, L0_attn_conv_b, L0_attn_x_proj_W, L0_attn_dt_proj_W,
           L0_attn_dt_proj_b, L0_attn_A_log, L0_attn_D_param,
           L0_attn_out_proj_W, L0_ffn_w1, L0_ffn_w2, L0_ffn_w3):
    f32 = lambda a: np.ascontiguousarray(np.asarray(a), dtype=np.float32)
    x = f32(L0_attn_feat_patch)
    w1 = f32(L0_ffn_w1)
    w2 = f32(L0_ffn_w2)
    w3 = f32(L0_ffn_w3)

    def wtiles(w):
        t = w.T.reshape(NCC, 128, NH, 128).transpose(2, 1, 0, 3)
        return np.ascontiguousarray(t).reshape(NH, 128, NCC * 128)

    shared = {
        "w1L": wtiles(w1),
        "w3L": wtiles(w3),
        "w2T": np.ascontiguousarray(w2.T).reshape(NH, 128, C),
        "inprojT": np.ascontiguousarray(f32(L0_attn_in_proj_W).T),
        "outprojT": np.ascontiguousarray(f32(L0_attn_out_proj_W).T),
        "xprojT": np.ascontiguousarray(f32(L0_attn_x_proj_W).T),
        "dtT": np.ascontiguousarray(f32(L0_attn_dt_proj_W).T),
        "Alog": f32(L0_attn_A_log),
        "Dp": f32(L0_attn_D_param).reshape(D, 1),
        "convw": f32(L0_attn_conv_w).reshape(D, 3),
        "convb": f32(L0_attn_conv_b).reshape(D, 1),
        "dtb": f32(L0_attn_dt_proj_b).reshape(D, 1),
        "lnw": np.ascontiguousarray(f32(L0_attn_ln_w).reshape(NCC, 128).T),
        "lnb": np.ascontiguousarray(f32(L0_attn_ln_b).reshape(NCC, 128).T),
    }
    in_maps = []
    for i in range(8):
        m = dict(shared)
        m["x"] = np.ascontiguousarray(x[i])
        in_maps.append(m)

    nc = _get_prog()
    res = run_bass_kernel_spmd(nc, in_maps, core_ids=list(range(8)))
    return np.stack([np.asarray(r["out"], dtype=np.float32)
                     for r in res.results], axis=0)
